# version 01
# baseline (speedup 1.0000x reference)
import math
from contextlib import ExitStack

import numpy as np
import concourse.bass as bass
import concourse.mybir as mybir
from concourse.bass_utils import run_bass_kernel_spmd

F32 = mybir.dt.float32
BF16 = mybir.dt.bfloat16
AF = mybir.ActivationFunctionType
ALU = mybir.AluOpType
AX = mybir.AxisListType

NC = 8
B = 4
S = 4096
D = 2048
T = D // 128
SL = S // NC
RW = B * SL
H = 16
HPC = H // NC
EPS = 1e-5
NEGBIG = -3.0e10

MLA_NOPE = 128
MLA_ROPE = 64
MLA_QH = 192
MLA_RANK = 512
FDK = 1024
FHDK = 64

DEBUG_TAPS = []


class SemCtr:
    def __init__(self, nc, name):
        self._cm = nc.semaphore(name)
        self.h = self._cm.__enter__()
        self.n = 0

    def inc(self, k=1):
        self.n += k
        return self.h

    def tick(self, k=1):
        self.n += k
        return (self.h, k)


class VS:

    def __init__(self, cx):
        self._cx = cx

    def __getattr__(self, name):
        cx = self._cx
        f = getattr(cx.nc.vector, name)
        if name in ("wait_ge", "partition_id"):
            return f

        def g(*a, **k):
            ins = f(*a, **k)
            ins.then_inc(*cx.ve.tick())
            cx.nc.vector.wait_ge(cx.ve.h, cx.ve.n)
            return ins
        return g


class Ctx:

    def __init__(self, nc):
        self.nc = nc
        self.es = ExitStack()
        self.gw = SemCtr(nc, "gw")
        self.sd = SemCtr(nc, "sd")
        self.ad = SemCtr(nc, "ad")
        self.cc = SemCtr(nc, "cc")
        self.pe = SemCtr(nc, "pe")
        self.ve = SemCtr(nc, "ve")
        self.ac = SemCtr(nc, "ac")
        self.ge = SemCtr(nc, "ge")

    def barrier(self):
        self.nc.all_engine_barrier()

    def sb(self, name, shape, dtype):
        return self.es.enter_context(self.nc.sbuf_tensor(name, shape, dtype))

    def ps(self, name, shape, dtype=F32):
        return self.es.enter_context(self.nc.psum_tensor(name, shape, dtype))


_UNIQ = [0]


def un(nm):
    _UNIQ[0] += 1
    return f"{nm}_{_UNIQ[0]}"


def dram(nc, name, shape, dtype, kind="Internal"):
    return nc.dram_tensor(name, shape, dtype, kind=kind).ap()



def emit_ln(cx, st, xres, consts, pad_rows=0):
    nc = cx.nc
    ln = st["ln"]
    ones1 = consts["ones1"]
    onesr = consts["onesr_f"]
    CH = 512
    NCH = RW // CH
    with ExitStack() as es:
        xt = es.enter_context(nc.sbuf_tensor(un("ln_xt"), [128, T, CH], BF16))
        sq = es.enter_context(nc.sbuf_tensor(un("ln_sq"), [128, 2, CH], BF16))
        stats = es.enter_context(nc.sbuf_tensor(un("ln_stats"), [1, 6, CH], F32))
        p_sum = es.enter_context(nc.psum_tensor(un("ln_psum"), [1, 2, CH], F32))
        p_bc = es.enter_context(nc.psum_tensor(un("ln_bc"), [128, 2, CH], F32))
        tmf = es.enter_context(nc.sbuf_tensor(un("ln_tmf"), [128, CH], F32))
        if pad_rows:
            cx.nc.vector.memset(ln[:, :, 0:pad_rows], 0.0).then_inc(*cx.ve.tick())
        for c in range(NCH):
            r0 = c * CH
            for t in range(T):
                nc.sync.dma_start(out=xt[:, t, :], in_=xres[t, :, r0:r0 + CH]
                                  ).then_inc(*cx.sd.tick(16))
            nc.tensor.wait_ge(cx.sd.h, cx.sd.n)
            for t in range(T):
                nc.tensor.matmul(p_sum[:, 0, :], ones1[:], xt[:, t, :],
                                 start=(t == 0), stop=(t == T - 1))
            nc.scalar.wait_ge(cx.sd.h, cx.sd.n)
            pe_base = cx.pe.n
            for t in range(T):
                if t >= 2:
                    nc.scalar.wait_ge(cx.pe.h, pe_base + t - 1)
                nc.scalar.activation(sq[:, t % 2, :], xt[:, t, :], AF.Square
                                     ).then_inc(*cx.ac.tick())
                nc.tensor.wait_ge(cx.ac.h, cx.ac.n)
                nc.tensor.matmul(p_sum[:, 1, :], ones1[:], sq[:, t % 2, :],
                                 start=(t == 0), stop=(t == T - 1)
                                 ).then_inc(*cx.pe.tick())
            v = nc.vector
            v.wait_ge(cx.pe.h, cx.pe.n)
            v.tensor_scalar_mul(stats[:, 0, :], p_sum[:, 0, :], 1.0 / D)
            v.tensor_scalar_mul(stats[:, 1, :], p_sum[:, 1, :], 1.0 / D)
            v.tensor_mul(stats[:, 2, :], stats[:, 0, :], stats[:, 0, :])
            v.tensor_sub(stats[:, 2, :], stats[:, 1, :], stats[:, 2, :])
            v.tensor_scalar_add(stats[:, 2, :], stats[:, 2, :], EPS
                                ).then_inc(*cx.ve.tick())
            nc.scalar.wait_ge(cx.ve.h, cx.ve.n)
            nc.scalar.activation(stats[:, 4, :], stats[:, 2, :], AF.Sqrt
                                 ).then_inc(*cx.ac.tick())
            v.wait_ge(cx.ac.h, cx.ac.n)
            v.reciprocal(stats[:, 3, :], stats[:, 4, :])
            v.tensor_mul(stats[:, 4, :], stats[:, 3, :], stats[:, 3, :])
            v.tensor_mul(stats[:, 4, :], stats[:, 4, :], stats[:, 2, :])
            v.tensor_scalar(stats[:, 4, :], stats[:, 4, :], -0.5, 1.5,
                            ALU.mult, ALU.add)
            v.tensor_mul(stats[:, 3, :], stats[:, 3, :], stats[:, 4, :])
            v.tensor_scalar_mul(stats[:, 0, :], stats[:, 0, :], -1.0
                                ).then_inc(*cx.ve.tick())
            nc.tensor.wait_ge(cx.ve.h, cx.ve.n)
            nc.tensor.matmul(p_bc[:, 0, :], onesr[:], stats[:, 0, :],
                             start=True, stop=True)
            nc.tensor.matmul(p_bc[:, 1, :], onesr[:], stats[:, 3, :],
                             start=True, stop=True).then_inc(*cx.pe.tick())
            v.wait_ge(cx.pe.h, cx.pe.n)
            for t in range(T):
                v.tensor_add(tmf[:], xt[:, t, :], p_bc[:, 0, :])
                ins = v.tensor_mul(ln[:, t, pad_rows + r0:pad_rows + r0 + CH],
                                   tmf[:], p_bc[:, 1, :])
            ins.then_inc(*cx.ve.tick())
            nc.sync.wait_ge(cx.ve.h, cx.ve.n)
            nc.tensor.wait_ge(cx.ve.h, cx.ve.n)


def emit_qkv_a(cx, ln, inp, aq_in, ak_in, av_in):
    nc = cx.nc
    with ExitStack() as es:
        wb = es.enter_context(nc.sbuf_tensor(un("qa_w"), [128, 2, T, 128], BF16))
        bias = es.enter_context(nc.sbuf_tensor(un("qa_b"), [128, 2 * H], F32))
        stg = es.enter_context(nc.sbuf_tensor(un("qa_stg"), [128, 2, 512], BF16))
        psA = es.enter_context(nc.psum_tensor(un("qa_ps"), [128, 2, 512], F32))
        nc.gpsimd.dma_start(out=bias[:, 0:H], in_=inp["bq_a"]
                            ).then_inc(*cx.gw.tick(16))
        nc.gpsimd.dma_start(out=bias[:, H:2 * H], in_=inp["bk_a"]
                            ).then_inc(*cx.gw.tick(16))
        mm_done = {}
        evac_done = {}
        for i in range(2 * H):
            kind, h = divmod(i, H)
            wsrc = inp["wq_a"] if kind == 0 else inp["wk_a"]
            dst = aq_in if kind == 0 else ak_in
            buf = i % 2
            if i >= 2:
                nc.gpsimd.wait_ge(cx.pe.h, mm_done[i - 2])
            nc.gpsimd.dma_start(
                out=wb[:, buf, :, :],
                in_=wsrc[h].rearrange("t q e -> q t e")).then_inc(*cx.gw.tick(16))
            nc.tensor.wait_ge(cx.gw.h, cx.gw.n)
            for nch in range(4):
                if i * 4 + nch >= 2:
                    nc.tensor.wait_ge(cx.ac.h, evac_done.get(i * 4 + nch - 2, 0))
                for t in range(T):
                    ins = nc.tensor.matmul(
                        psA[:, nch % 2, :], wb[:, buf, t, :],
                        ln[:, t, nch * 512:(nch + 1) * 512],
                        start=(t == 0), stop=(t == T - 1))
                ins.then_inc(*cx.pe.tick())
                nc.scalar.wait_ge(cx.pe.h, cx.pe.n)
                nc.scalar.activation(
                    stg[:, nch % 2, :], psA[:, nch % 2, :], AF.Identity,
                    bias=bias[:, i:i + 1], scale=1.0).then_inc(*cx.ac.tick())
                evac_done[i * 4 + nch] = cx.ac.n
                nc.sync.wait_ge(cx.ac.h, cx.ac.n)
                nc.sync.dma_start(
                    out=dst[h * 128:(h + 1) * 128, nch * 512:(nch + 1) * 512],
                    in_=stg[:, nch % 2, :]).then_inc(*cx.sd.tick(16))
                nc.scalar.wait_ge(cx.sd.h, cx.sd.n - 16)
            mm_done[i] = cx.pe.n
        nc.sync.wait_ge(cx.sd.h, cx.sd.n)
        cx.barrier()

    with ExitStack() as es:
        wv = es.enter_context(nc.sbuf_tensor(un("qa_wv"), [128, T, D], BF16))
        bv = es.enter_context(nc.sbuf_tensor(un("qa_bv"), [128, D], F32))
        stg = es.enter_context(nc.sbuf_tensor(un("qa_stgv"), [128, 2, 512], BF16))
        psV = es.enter_context(nc.psum_tensor(un("qa_psv"), [128, 4, 512], F32))
        for t in range(T):
            nc.gpsimd.dma_start(out=wv[:, t, :], in_=inp["wv_a"][t]
                                ).then_inc(*cx.gw.tick(16))
        nc.gpsimd.dma_start(out=bv[:], in_=inp["bv_a"]).then_inc(*cx.gw.tick(16))
        nc.tensor.wait_ge(cx.gw.h, cx.gw.n)
        nc.vector.wait_ge(cx.gw.h, cx.gw.n)
        evac_ve = {}
        for rc in range(T):
            for nch in range(4):
                j = rc * 4 + nch
                if j >= 4:
                    nc.tensor.wait_ge(cx.ve.h, evac_ve[j - 4])
                for t in range(T):
                    ins = nc.tensor.matmul(
                        psV[:, nch, :], ln[:, t, rc * 128:(rc + 1) * 128],
                        wv[:, t, nch * 512:(nch + 1) * 512],
                        start=(t == 0), stop=(t == T - 1))
                ins.then_inc(*cx.pe.tick())
                nc.vector.wait_ge(cx.pe.h, cx.pe.n)
                if j >= 2:
                    nc.vector.wait_ge(cx.sd.h, cx.sd.n - 16)
                nc.vector.tensor_add(stg[:, j % 2, :], psV[:, nch, :],
                                     bv[:, nch * 512:(nch + 1) * 512]
                                     ).then_inc(*cx.ve.tick())
                evac_ve[j] = cx.ve.n
                nc.sync.wait_ge(cx.ve.h, cx.ve.n)
                nc.sync.dma_start(
                    out=av_in[rc * 128:(rc + 1) * 128, nch * 512:(nch + 1) * 512],
                    in_=stg[:, j % 2, :]).then_inc(*cx.sd.tick(16))
        nc.sync.wait_ge(cx.sd.h, cx.sd.n)
        cx.barrier()


def emit_attn_inner_a(cx, consts, aq_my, ak_my, av_my, az_in):
    nc = cx.nc
    sc = 1.0 / math.sqrt(128.0)
    with ExitStack() as es:
        kb_ = es.enter_context(nc.sbuf_tensor(un("aa_k"), [128, 2, 32, 128], BF16))
        qb_ = es.enter_context(nc.sbuf_tensor(un("aa_q"), [128, 2, 32, 128], BF16))
        vb_ = es.enter_context(nc.sbuf_tensor(un("aa_v"), [128, 2, 32, 128], BF16))
        pt = es.enter_context(nc.sbuf_tensor(un("aa_p"), [128, 2, 512], BF16))
        sC = es.enter_context(nc.sbuf_tensor(un("aa_c"), [128, 1], F32))
        sml = es.enter_context(nc.sbuf_tensor(un("aa_sml"), [128, 4], F32))
        qss = es.enter_context(nc.sbuf_tensor(un("aa_qss"), [128, 2, 128], BF16))
        rcp = es.enter_context(nc.sbuf_tensor(un("aa_rcp"), [128, 512], F32))
        zt = es.enter_context(nc.sbuf_tensor(un("aa_z"), [128, 2, 512], BF16))
        psS = es.enter_context(nc.psum_tensor(un("aa_psS"), [128, 2, 512], F32))
        psO = es.enter_context(nc.psum_tensor(un("aa_psO"), [128, 512], F32))
        psB = es.enter_context(nc.psum_tensor(un("aa_psB"), [128, 512], F32))
        psR = es.enter_context(nc.psum_tensor(un("aa_psR"), [128, 128], F32))
        psN = es.enter_context(nc.psum_tensor(un("aa_psN"), [1, 512], F32))
        ones1 = consts["ones1"]
        onesr = consts["onesr_f"]
        ident = consts["ident"]
        diag = consts["diag"]
        pe_end = {}
        ld_end = {}
        for inst in range(8):
            b, hp = divmod(inst, 2)
            buf = inst % 2
            if inst >= 2:
                nc.sync.wait_ge(cx.pe.h, pe_end[inst - 2])
            for j in range(NC):
                nc.sync.dma_start(
                    out=qb_[:, buf, j * 4:(j + 1) * 4, :].rearrange(
                        "p c e -> p (c e)"),
                    in_=aq_my[j, hp * 128:(hp + 1) * 128,
                              b * 512:(b + 1) * 512]).then_inc(*cx.sd.tick(16))
                nc.sync.dma_start(
                    out=kb_[:, buf, j * 4:(j + 1) * 4, :].rearrange(
                        "p c e -> p (c e)"),
                    in_=ak_my[j, hp * 128:(hp + 1) * 128,
                              b * 512:(b + 1) * 512]).then_inc(*cx.sd.tick(16))
                for cc_ in range(4):
                    r0 = b * 512 + cc_ * 128
                    nc.sync.dma_start(
                        out=vb_[:, buf, j * 4 + cc_, :],
                        in_=av_my[j, r0:r0 + 128, hp * 128:(hp + 1) * 128]
                    ).then_inc(*cx.sd.tick(16))
            ld_end[inst] = cx.sd.n
            nc.vector.wait_ge(cx.sd.h, ld_end[inst])
            if inst >= 1:
                nc.vector.wait_ge(cx.pe.h, cx.pe.n)
            nc.vector.tensor_copy(qss[:, 0, :], qb_[:, buf, :, 0:4])
            nc.vector.tensor_copy(qss[:, 1, :], kb_[:, buf, :, 0:4]
                                  ).then_inc(*cx.ve.tick())
            nc.tensor.wait_ge(cx.sd.h, ld_end[inst])
            nc.tensor.wait_ge(cx.ve.h, cx.ve.n)
            nc.tensor.matmul(psR[:], qss[:, 0, :], qss[:, 1, :],
                             start=True, stop=True).then_inc(*cx.pe.tick())
            nc.vector.wait_ge(cx.pe.h, cx.pe.n)
            nc.vector.tensor_reduce(sml[:, 0:1], psR[:], AX.X, ALU.max
                                    ).then_inc(*cx.ve.tick())
            nc.tensor.wait_ge(cx.ve.h, cx.ve.n)
            nc.tensor.transpose(psR[0:1, :], sml[:, 0:1], consts["ident_f"][:]
                                ).then_inc(*cx.pe.tick())
            nc.vector.wait_ge(cx.pe.h, cx.pe.n)
            nc.vector.tensor_reduce(sml[0:1, 1:2], psR[0:1, :], AX.X, ALU.max)
            nc.vector.tensor_scalar(sml[0:1, 2:3], sml[0:1, 1:2], -sc, -2.0 * sc,
                                    ALU.mult, ALU.add).then_inc(*cx.ve.tick())
            nc.tensor.wait_ge(cx.ve.h, cx.ve.n)
            nc.tensor.matmul(psR[:, 0:1], onesr[:], sml[0:1, 2:3],
                             start=True, stop=True).then_inc(*cx.pe.tick())
            nc.vector.wait_ge(cx.pe.h, cx.pe.n)
            nc.vector.wait_ge(cx.ac.h, cx.ac.n)
            nc.vector.tensor_copy(sC[:], psR[:, 0:1]).then_inc(*cx.ve.tick())
            nc.scalar.wait_ge(cx.ve.h, cx.ve.n)
            for j in range(8):
                nk = 4 * j + 4
                for k in range(nk):
                    sb = k % 2
                    if k >= 2:
                        nc.tensor.wait_ge(cx.ac.h, cx.ac.n - 1)
                    ins = nc.tensor.matmul(
                        psS[:, sb, :], kb_[:, buf, k, :],
                        qb_[:, buf, j * 4:(j + 1) * 4, :].rearrange(
                            "p c e -> p (c e)"),
                        start=True, stop=True, skip_group_check=True)
                    ins.then_inc(*cx.pe.tick())
                    v_of_k = k - 4 * j
                    if 0 <= v_of_k < 4:
                        nc.vector.wait_ge(cx.pe.h, cx.pe.n)
                        nc.vector.tensor_tensor(
                            psS[:, sb, :], psS[:, sb, :], diag[:, v_of_k, :],
                            ALU.add).then_inc(*cx.ve.tick())
                        nc.scalar.wait_ge(cx.ve.h, cx.ve.n)
                    else:
                        nc.scalar.wait_ge(cx.pe.h, cx.pe.n)
                    if k >= 2:
                        nc.scalar.wait_ge(cx.pe.h, cx.pe.n)
                    nc.scalar.activation(pt[:, sb, :], psS[:, sb, :], AF.Exp,
                                         bias=sC[:, 0:1], scale=sc
                                         ).then_inc(*cx.ac.tick())
                    nc.tensor.wait_ge(cx.ac.h, cx.ac.n)
                    nc.tensor.matmul(psO[:], vb_[:, buf, k, :], pt[:, sb, :],
                                     start=(k == 0), stop=(k == nk - 1),
                                     skip_group_check=True)
                    nc.tensor.matmul(psN[:], ones1[:], pt[:, sb, :],
                                     start=(k == 0), stop=(k == nk - 1),
                                     skip_group_check=True).then_inc(*cx.pe.tick())
                nc.vector.wait_ge(cx.pe.h, cx.pe.n)
                nc.vector.reciprocal(rcp[0:1, :], psN[:]).then_inc(*cx.ve.tick())
                nc.tensor.wait_ge(cx.ve.h, cx.ve.n)
                nc.tensor.matmul(psB[:], onesr[:], rcp[0:1, :],
                                 start=True, stop=True).then_inc(*cx.pe.tick())
                nc.vector.wait_ge(cx.pe.h, cx.pe.n)
                if j >= 2:
                    nc.vector.wait_ge(cx.sd.h, cx.sd.n)
                nc.vector.tensor_copy(rcp[:], psB[:])
                nc.vector.tensor_mul(zt[:, j % 2, :], psO[:], rcp[:]
                                     ).then_inc(*cx.ve.tick())
                nc.sync.wait_ge(cx.ve.h, cx.ve.n)
                nc.sync.dma_start(
                    out=az_in[hp * 128:(hp + 1) * 128,
                              b * 4096 + j * 512:b * 4096 + (j + 1) * 512],
                    in_=zt[:, j % 2, :]).then_inc(*cx.sd.tick(16))
                nc.tensor.wait_ge(cx.ve.h, cx.ve.n)
            pe_end[inst] = cx.pe.n
        nc.sync.wait_ge(cx.sd.h, cx.sd.n)
        cx.barrier()


def emit_outproj(cx, w_ap, bias_ap, z_all, xres, out_fp32=None):
    nc = cx.nc
    with ExitStack() as es:
        zs = es.enter_context(nc.sbuf_tensor(un("op_z"), [128, H, B, 512], BF16))
        wo = es.enter_context(nc.sbuf_tensor(un("op_w"), [128, 2, H, 128], BF16))
        bo = es.enter_context(nc.sbuf_tensor(un("op_b"), [128, T], F32))
        xr = es.enter_context(nc.sbuf_tensor(un("op_xr"), [128, 2, 512], BF16))
        stg = es.enter_context(nc.sbuf_tensor(un("op_stg"), [128, 2, 512], BF16))
        stgf = es.enter_context(nc.sbuf_tensor(un("op_stgf"), [128, 2, 512], F32))
        psO = es.enter_context(nc.psum_tensor(un("op_ps"), [128, 2, 512], F32))
        if bias_ap is not None:
            nc.gpsimd.dma_start(out=bo[:], in_=bias_ap).then_inc(*cx.gw.tick(16))
        for h in range(H):
            nc.scalar.dma_start(
                out=zs[:, h, :, :],
                in_=z_all[h * 128:(h + 1) * 128, :, :]).then_inc(*cx.ad.tick(16))
        nc.tensor.wait_ge(cx.ad.h, cx.ad.n)
        ev_done = {}
        for dt in range(T):
            dbuf = dt % 2
            if dt >= 2:
                nc.gpsimd.wait_ge(cx.pe.h, ev_done.get(("mm", dt - 2), 0))
            nc.gpsimd.dma_start(
                out=wo[:, dbuf, :, :],
                in_=w_ap[:, dt, :, :].rearrange("h e q -> e h q")
            ).then_inc(*cx.gw.tick(16))
            nc.tensor.wait_ge(cx.gw.h, cx.gw.n)
            for b_ in range(B):
                jj = dt * B + b_
                if jj >= 2:
                    nc.tensor.wait_ge(cx.ve.h, ev_done.get(("ev", jj - 2), 0))
                for h in range(H):
                    ins = nc.tensor.matmul(
                        psO[:, jj % 2, :], wo[:, dbuf, h, :], zs[:, h, b_, :],
                        start=(h == 0), stop=(h == H - 1))
                ins.then_inc(*cx.pe.tick())
                ev_done[("mm", dt)] = cx.pe.n
                nc.sync.dma_start(
                    out=xr[:, jj % 2, :],
                    in_=xres[dt, :, b_ * 512:(b_ + 1) * 512]
                ).then_inc(*cx.sd.tick(16))
                nc.vector.wait_ge(cx.pe.h, cx.pe.n)
                nc.vector.wait_ge(cx.sd.h, cx.sd.n)
                if jj >= 2:
                    pass
                if bias_ap is not None:
                    nc.vector.scalar_tensor_tensor(
                        stg[:, jj % 2, :], psO[:, jj % 2, :], bo[:, dt:dt + 1],
                        xr[:, jj % 2, :], ALU.add, ALU.add)
                else:
                    nc.vector.tensor_add(stg[:, jj % 2, :], psO[:, jj % 2, :],
                                         xr[:, jj % 2, :])
                if out_fp32 is not None:
                    if bias_ap is not None:
                        nc.vector.scalar_tensor_tensor(
                            stgf[:, jj % 2, :], psO[:, jj % 2, :],
                            bo[:, dt:dt + 1], xr[:, jj % 2, :], ALU.add, ALU.add)
                    else:
                        nc.vector.tensor_add(stgf[:, jj % 2, :],
                                             psO[:, jj % 2, :], xr[:, jj % 2, :])
                nc.vector.engine_nop().then_inc(*cx.ve.tick())
                ev_done[("ev", jj)] = cx.ve.n
                nc.sync.wait_ge(cx.ve.h, cx.ve.n)
                nc.sync.dma_start(
                    out=xres[dt, :, b_ * 512:(b_ + 1) * 512],
                    in_=stg[:, jj % 2, :]).then_inc(*cx.sd.tick(16))
                if out_fp32 is not None:
                    nc.sync.dma_start(
                        out=out_fp32[dt, :, b_ * 512:(b_ + 1) * 512],
                        in_=stgf[:, jj % 2, :]).then_inc(*cx.sd.tick(16))
        nc.sync.wait_ge(cx.sd.h, cx.sd.n)
        cx.barrier()


def emit_ffn(cx, consts, inp, xres, vffn, out_fp32=None, stop_after=None,
             dbg=None):
    def dump(name, sb_ap):
        if dbg and name in dbg:
            nc = cx.nc
            nc.gpsimd.dma_start(out=dbg[name], in_=sb_ap).then_inc(*cx.gw.tick(16))
            nc.gpsimd.wait_ge(cx.gw.h, cx.gw.n)
            cx.barrier()
    """Whole FFN sublayer: LN -> conv -> v/router -> q,k/rope/dots -> o/LN ->
    gate -> Wout + residual. All row-local."""
    nc = cx.nc
    onesr = consts["onesr_f"]
    identb = consts["ident"]
    with ExitStack() as big:
        xmix = big.enter_context(nc.sbuf_tensor(un("f_xmix"), [128, T, RW], BF16))
        with ExitStack() as es:
            ln = es.enter_context(nc.sbuf_tensor(un("f_ln"), [128, T, 512 + RW], BF16))
            emit_ln(cx, {"ln": ln}, xres, consts, pad_rows=512)
            cx.barrier()
            cw = es.enter_context(nc.sbuf_tensor(un("f_cw"), [128, T, 2], F32))
            pre = es.enter_context(nc.sbuf_tensor(un("f_pre"), [128, 2, 512], F32))
            nc.gpsimd.dma_start(out=cw[:], in_=inp["f_conv"].rearrange(
                "t q e -> q t e")).then_inc(*cx.gw.tick(16))
            nc.vector.wait_ge(cx.gw.h, cx.gw.n)
            for t in range(T):
                for ch in range(4):
                    r0 = ch * 512
                    i = t * 4 + ch
                    if i >= 2:
                        nc.vector.wait_ge(cx.ac.h, cx.ac.n - 1)
                    nc.vector.tensor_scalar_mul(
                        pre[:, i % 2, :], ln[:, t, 512 + r0:1024 + r0],
                        cw[:, t, 1:2])
                    nc.vector.scalar_tensor_tensor(
                        pre[:, i % 2, :], ln[:, t, r0:r0 + 512], cw[:, t, 0:1],
                        pre[:, i % 2, :], ALU.mult, ALU.add
                    ).then_inc(*cx.ve.tick())
                    nc.scalar.wait_ge(cx.ve.h, cx.ve.n)
                    nc.scalar.activation(xmix[:, t, r0:r0 + 512], pre[:, i % 2, :],
                                         AF.Silu).then_inc(*cx.ac.tick())
            nc.scalar.wait_ge(cx.ac.h, cx.ac.n)
            cx.barrier()

        if stop_after == "conv":
            return
        rwrep = big.enter_context(nc.sbuf_tensor(un("f_rwrep"), [128, D], F32))
        ftd = big.enter_context(nc.sbuf_tensor(un("f_ftd"), [128, 16, H], F32))
        with ExitStack() as es:
            rw2 = es.enter_context(nc.sbuf_tensor(un("f_rw2"), [1, D, 2], F32))
            rwrow = es.enter_context(nc.sbuf_tensor(un("f_rwrow"), [1, D], F32))
            psB = es.enter_context(nc.psum_tensor(un("f_psb"), [128, 512], F32))
            nc.gpsimd.dma_start(out=rw2[:], in_=inp["f_rw"].rearrange(
                "h j e -> (h j) e")[None, :, :]).then_inc(*cx.gw.tick(16))
            nc.vector.wait_ge(cx.gw.h, cx.gw.n)
            nc.vector.tensor_sub(rwrow[:], rw2[:, :, 0], rw2[:, :, 1]
                                 ).then_inc(*cx.ve.tick())
            for nch in range(4):
                nc.tensor.wait_ge(cx.ve.h, cx.ve.n)
                nc.tensor.matmul(psB[:], onesr[:],
                                 rwrow[:, nch * 512:(nch + 1) * 512],
                                 start=True, stop=True).then_inc(*cx.pe.tick())
                nc.vector.wait_ge(cx.pe.h, cx.pe.n)
                nc.vector.tensor_copy(rwrep[:, nch * 512:(nch + 1) * 512],
                                      psB[:]).then_inc(*cx.ve.tick())
            nc.tensor.wait_ge(cx.ve.h, cx.ve.n)
            cx.barrier()

        if stop_after == "rw":
            return
        with ExitStack() as es:
            wv = es.enter_context(nc.sbuf_tensor(un("f_wv"), [128, T, D], BF16))
            stg = es.enter_context(nc.sbuf_tensor(un("f_stgv"), [128, 2, 512], BF16))
            dtm = es.enter_context(nc.sbuf_tensor(un("f_dtm"), [128, 512], F32))
            psV = es.enter_context(nc.psum_tensor(un("f_psv"), [128, 4, 512], F32))
            for t in range(T):
                nc.gpsimd.dma_start(out=wv[:, t, :], in_=inp["f_wv"][t]
                                    ).then_inc(*cx.gw.tick(16))
            nc.tensor.wait_ge(cx.gw.h, cx.gw.n)
            evac_ve = {}
            for rc in range(T):
                for nch in range(4):
                    j = rc * 4 + nch
                    if j >= 4:
                        nc.tensor.wait_ge(cx.ve.h, evac_ve[j - 4])
                    for t in range(T):
                        ins = nc.tensor.matmul(
                            psV[:, nch, :], xmix[:, t, rc * 128:(rc + 1) * 128],
                            wv[:, t, nch * 512:(nch + 1) * 512],
                            start=(t == 0), stop=(t == T - 1))
                    ins.then_inc(*cx.pe.tick())
                    nc.vector.wait_ge(cx.pe.h, cx.pe.n)
                    if j >= 2:
                        nc.vector.wait_ge(cx.sd.h, cx.sd.n - 16)
                    nc.vector.tensor_copy(stg[:, j % 2, :], psV[:, nch, :])
                    nc.vector.tensor_mul(dtm[:], psV[:, nch, :],
                                         rwrep[:, nch * 512:(nch + 1) * 512])
                    nc.vector.tensor_reduce(
                        ftd[:, rc, nch * 4:(nch + 1) * 4],
                        dtm[:].rearrange("p (h j) -> p h j", j=128),
                        AX.X, ALU.add).then_inc(*cx.ve.tick())
                    evac_ve[j] = cx.ve.n
                    nc.sync.wait_ge(cx.ve.h, cx.ve.n)
                    nc.sync.dma_start(
                        out=vffn[rc * 128:(rc + 1) * 128,
                                 nch * 512:(nch + 1) * 512],
                        in_=stg[:, j % 2, :]).then_inc(*cx.sd.tick(16))
            nc.sync.wait_ge(cx.sd.h, cx.sd.n)
            cx.barrier()

        if stop_after == "v":
            return
        CF = big.enter_context(nc.sbuf_tensor(un("f_cf"), [128, 4, 4, 4, H], F32))
        with ExitStack() as es:
            fq = es.enter_context(nc.sbuf_tensor(un("f_q"), [128, T, FDK], BF16))
            fk = es.enter_context(nc.sbuf_tensor(un("f_k"), [128, T, FDK], BF16))
            with ExitStack() as esw:
                wqk = esw.enter_context(nc.sbuf_tensor(un("f_wqk"), [128, T, FDK], BF16))
                psQ = esw.enter_context(nc.psum_tensor(un("f_psq"), [128, 4, 512], F32))
                for kind in range(2):
                    wsrc_ap = inp["f_wq"] if kind == 0 else inp["f_wk"]
                    dst = fq if kind == 0 else fk
                    if kind == 1:
                        nc.gpsimd.wait_ge(cx.ac.h, cx.ac.n)
                    for t in range(T):
                        nc.gpsimd.dma_start(out=wqk[:, t, :], in_=wsrc_ap[t]
                                            ).then_inc(*cx.gw.tick(16))
                    nc.tensor.wait_ge(cx.gw.h, cx.gw.n)
                    done = {}
                    for rc in range(T):
                        for half in range(2):
                            j = rc * 2 + half
                            bank = j % 4
                            if j >= 4:
                                nc.tensor.wait_ge(cx.ac.h, done[j - 4])
                            for t in range(T):
                                ins = nc.tensor.matmul(
                                    psQ[:, bank, :],
                                    xmix[:, t, rc * 128:(rc + 1) * 128],
                                    wqk[:, t, half * 512:(half + 1) * 512],
                                    start=(t == 0), stop=(t == T - 1))
                            ins.then_inc(*cx.pe.tick())
                            nc.scalar.wait_ge(cx.pe.h, cx.pe.n)
                            nc.scalar.activation(
                                dst[:, rc, half * 512:(half + 1) * 512],
                                psQ[:, bank, :],
                                AF.Silu if kind == 0 else AF.Identity
                            ).then_inc(*cx.ac.tick())
                            done[j] = cx.ac.n
            nc.scalar.wait_ge(cx.ac.h, cx.ac.n)
            cx.barrier()
            cosr = es.enter_context(nc.sbuf_tensor(un("f_cos"), [128, B, H * 32], F32))
            sinr = es.enter_context(nc.sbuf_tensor(un("f_sin"), [128, B, H * 32], F32))
            s1 = es.enter_context(nc.sbuf_tensor(un("f_s1"), [128, 2, H * 32], F32))
            nc.gpsimd.dma_start(out=cosr[:], in_=inp["ffn_cos"].rearrange(
                "b q c -> q b c")).then_inc(*cx.gw.tick(16))
            nc.gpsimd.dma_start(out=sinr[:], in_=inp["ffn_sin"].rearrange(
                "b q c -> q b c")).then_inc(*cx.gw.tick(16))
            nc.vector.wait_ge(cx.gw.h, cx.gw.n)
            v = VS(cx)
            for rc in range(T):
                b_ = rc // 4
                for buf_ in (fq, fk):
                    xh = buf_[:, rc, :].rearrange("p (h d) -> p h d", d=64)
                    x1 = xh[:, :, 0:32]
                    x2 = xh[:, :, 32:64]
                    cosv = cosr[:, b_, :].rearrange("p (h d) -> p h d", d=32)
                    sinv = sinr[:, b_, :].rearrange("p (h d) -> p h d", d=32)
                    s1v = s1[:, 0, :].rearrange("p (h d) -> p h d", d=32)
                    s2v = s1[:, 1, :].rearrange("p (h d) -> p h d", d=32)
                    v.tensor_mul(s1v, x1, cosv)
                    v.tensor_mul(s2v, x2, sinv)
                    v.tensor_sub(s1v, s1v, s2v)
                    v.tensor_mul(s2v, x1, sinv)
                    v.tensor_mul(x2, x2, cosv)
                    v.tensor_add(x2, x2, s2v)
                    v.tensor_copy(x1, s1v)
            cx.barrier()
            dump("fq", fq[:])
            dump("fk", fk[:])

            i0 = es.enter_context(nc.sbuf_tensor(un("f_i0"), [128, 16, H], F32))
            al = es.enter_context(nc.sbuf_tensor(un("f_al"), [128, 16, H], F32))
            cum = es.enter_context(nc.sbuf_tensor(un("f_cum"), [128, 4, 2, H], F32))
            tmp = es.enter_context(nc.sbuf_tensor(un("f_tmp"), [128, 4, H], F32))
            dtm = es.enter_context(nc.sbuf_tensor(un("f_dt2"), [128, FDK], F32))
            v = VS(cx)
            s = nc.scalar
            v.tensor_single_scalar(i0[:], ftd[:], 0.0, ALU.is_ge)
            nc.gpsimd.wait_ge(cx.ve.h, cx.ve.n)
            dump("fi0b", i0[:])
            s.wait_ge(cx.ve.h, cx.ve.n)
            s.activation(al[:], ftd[:], AF.Abs).then_inc(*cx.ac.tick())
            s.wait_ge(cx.ac.h, cx.ac.n)
            s.activation(al[:], al[:], AF.Sigmoid).then_inc(*cx.ac.tick())
            v.wait_ge(cx.ac.h, cx.ac.n)
            for g in range(4):
                for b_ in range(B):
                    rt = b_ * 4 + g
                    if b_ == 0:
                        v.tensor_copy(cum[:, g, 0, :], i0[:, rt, :])
                        v.tensor_scalar(cum[:, g, 1, :], i0[:, rt, :], -1.0, 1.0,
                                        ALU.mult, ALU.add)
                    else:
                        v.tensor_add(cum[:, g, 0, :], cum[:, g, 0, :],
                                     i0[:, rt, :])
                        v.tensor_scalar(tmp[:, 0, :], i0[:, rt, :], -1.0, 1.0,
                                        ALU.mult, ALU.add)
                        v.tensor_add(cum[:, g, 1, :], cum[:, g, 1, :],
                                     tmp[:, 0, :])
                    v.tensor_mul(tmp[:, 1, :], i0[:, rt, :], cum[:, g, 0, :])
                    v.tensor_scalar(tmp[:, 2, :], i0[:, rt, :], -1.0, 1.0,
                                    ALU.mult, ALU.add)
                    v.tensor_mul(tmp[:, 2, :], tmp[:, 2, :], cum[:, g, 1, :])
                    v.tensor_add(tmp[:, 1, :], tmp[:, 1, :], tmp[:, 2, :])
                    v.tensor_scalar_max(tmp[:, 1, :], tmp[:, 1, :], 1.0)
                    v.reciprocal(tmp[:, 1, :], tmp[:, 1, :])
                    v.tensor_mul(tmp[:, 1, :], tmp[:, 1, :], al[:, rt, :])
                    v.tensor_scalar_mul(al[:, rt, :], tmp[:, 1, :],
                                        1.0 / math.sqrt(FHDK))
            nc.gpsimd.wait_ge(cx.ve.h, cx.ve.n)
            dump("fi0c", i0[:])
            dump("falc", al[:])
            for g in range(4):
                for l in range(B):
                    for m in range(l + 1):
                        rl = l * 4 + g
                        rm = m * 4 + g
                        v.tensor_mul(dtm[:], fq[:, rl, :], fk[:, rm, :])
                        v.tensor_reduce(tmp[:, 3, :],
                                        dtm[:].rearrange("p (h d) -> p h d", d=64),
                                        AX.X, ALU.add)
                        v.tensor_mul(tmp[:, 0, :], i0[:, rl, :], i0[:, rm, :])
                        v.tensor_add(tmp[:, 1, :], i0[:, rl, :], i0[:, rm, :])
                        v.tensor_scalar(tmp[:, 1, :], tmp[:, 1, :], -1.0, 1.0,
                                        ALU.mult, ALU.add)
                        v.tensor_scalar(tmp[:, 0, :], tmp[:, 0, :], 2.0, None,
                                        ALU.mult)
                        v.tensor_add(tmp[:, 0, :], tmp[:, 0, :], tmp[:, 1, :])
                        v.tensor_mul(tmp[:, 3, :], tmp[:, 3, :], tmp[:, 0, :])
                        v.tensor_mul(CF[:, g, l, m, :], tmp[:, 3, :],
                                     al[:, rl, :])
            cx.barrier()

        if stop_after == "qk":
            return
        olnT = big.enter_context(nc.sbuf_tensor(un("f_olnT"), [128, T, RW], BF16))
        with ExitStack() as es:
            vt = es.enter_context(nc.sbuf_tensor(un("f_vt"), [128, B, D], BF16))
            ob = es.enter_context(nc.sbuf_tensor(un("f_o"), [128, D], F32))
            osq = es.enter_context(nc.sbuf_tensor(un("f_osq"), [128, D], F32))
            stats = es.enter_context(nc.sbuf_tensor(un("f_ost"), [128, 6, H], F32))
            opos = es.enter_context(nc.sbuf_tensor(un("f_opos"), [128, D], BF16))
            psT = es.enter_context(nc.psum_tensor(un("f_pst"), [128, 2, 1024], BF16))
            v = VS(cx)
            ve_copy = {}
            for g in range(4):
                nc.sync.wait_ge(cx.ve.h, cx.ve.n)
                for m in range(B):
                    nc.sync.dma_start(out=vt[:, m, :],
                                      in_=vffn[(m * 4 + g) * 128:
                                               (m * 4 + g + 1) * 128, :]
                                      ).then_inc(*cx.sd.tick(16))
                v.wait_ge(cx.sd.h, cx.sd.n)
                for l in range(B):
                    for m in range(l + 1):
                        cfb = CF[:, g, l, m, :].unsqueeze(2).broadcast_to(
                            [128, H, 128])
                        vth = vt[:, m, :].rearrange("p (h d) -> p h d", d=128)
                        if m == 0:
                            v.tensor_mul(ob[:].rearrange("p (h d) -> p h d",
                                                         d=128), vth, cfb)
                        else:
                            v.tensor_mul(osq[:].rearrange("p (h d) -> p h d",
                                                          d=128), vth, cfb)
                            v.tensor_add(ob[:], ob[:], osq[:])
                    obh = ob[:].rearrange("p (h d) -> p h d", d=128)
                    v.tensor_reduce(stats[:, 0, :], obh, AX.X, ALU.add)
                    v.tensor_scalar_mul(stats[:, 0, :], stats[:, 0, :], 1.0 / 128)
                    nc.scalar.wait_ge(cx.ve.h, cx.ve.n)
                    nc.scalar.activation(osq[:], ob[:], AF.Square
                                         ).then_inc(*cx.ac.tick())
                    v.wait_ge(cx.ac.h, cx.ac.n)
                    v.tensor_reduce(stats[:, 1, :],
                                    osq[:].rearrange("p (h d) -> p h d", d=128),
                                    AX.X, ALU.add)
                    v.tensor_scalar_mul(stats[:, 1, :], stats[:, 1, :], 1.0 / 128)
                    v.tensor_mul(stats[:, 2, :], stats[:, 0, :], stats[:, 0, :])
                    v.tensor_sub(stats[:, 2, :], stats[:, 1, :], stats[:, 2, :])
                    v.tensor_scalar_add(stats[:, 2, :], stats[:, 2, :], EPS)
                    nc.scalar.wait_ge(cx.ve.h, cx.ve.n)
                    nc.scalar.activation(stats[:, 4, :], stats[:, 2, :], AF.Sqrt
                                         ).then_inc(*cx.ac.tick())
                    v.wait_ge(cx.ac.h, cx.ac.n)
                    v.reciprocal(stats[:, 3, :], stats[:, 4, :])
                    v.tensor_mul(stats[:, 4, :], stats[:, 3, :], stats[:, 3, :])
                    v.tensor_mul(stats[:, 4, :], stats[:, 4, :], stats[:, 2, :])
                    v.tensor_scalar(stats[:, 4, :], stats[:, 4, :], -0.5, 1.5,
                                    ALU.mult, ALU.add)
                    v.tensor_mul(stats[:, 3, :], stats[:, 3, :], stats[:, 4, :])
                    mb = stats[:, 0, :].unsqueeze(2).broadcast_to([128, H, 128])
                    rb = stats[:, 3, :].unsqueeze(2).broadcast_to([128, H, 128])
                    v.tensor_sub(osq[:].rearrange("p (h d) -> p h d", d=128),
                                 obh, mb)
                    v.tensor_mul(opos[:].rearrange("p (h d) -> p h d", d=128),
                                 osq[:].rearrange("p (h d) -> p h d", d=128),
                                 rb)
                    rt = l * 4 + g
                    nc.tensor.wait_ge(cx.ve.h, cx.ve.n)
                    for h in range(H):
                        if h >= 2:
                            nc.tensor.wait_ge(cx.ve.h, ve_copy[h - 2])
                        nc.tensor.transpose(psT[:, h % 2, 0:128],
                                            opos[:, h * 128:(h + 1) * 128],
                                            identb[:]).then_inc(*cx.pe.tick())
                        v.wait_ge(cx.pe.h, cx.pe.n)
                        v.tensor_copy(olnT[:, h, rt * 128:(rt + 1) * 128],
                                      psT[:, h % 2, 0:128])
                        ve_copy[h] = cx.ve.n
            cx.barrier()

        if stop_after == "o":
            return
        dump("folnT", olnT[:])
        with ExitStack() as es:
            wg = es.enter_context(nc.sbuf_tensor(un("f_wg"), [128, 2, T, 128], BF16))
            sg = es.enter_context(nc.sbuf_tensor(un("f_sg"), [128, 2, 512], BF16))
            psG = es.enter_context(nc.psum_tensor(un("f_psg"), [128, 2, 512], F32))
            mm_done = {}
            ev_done = {}
            for gt in range(T):
                dbuf = gt % 2
                if gt >= 2:
                    nc.gpsimd.wait_ge(cx.pe.h, mm_done[gt - 2])
                nc.gpsimd.dma_start(
                    out=wg[:, dbuf, :, :],
                    in_=inp["f_wg"][:, :, gt, :].rearrange("t q e -> q t e")
                ).then_inc(*cx.gw.tick(16))
                nc.tensor.wait_ge(cx.gw.h, cx.gw.n)
                for nch in range(4):
                    jj = gt * 4 + nch
                    if jj >= 2:
                        nc.tensor.wait_ge(cx.ve.h, ev_done.get(jj - 2, 0))
                    for t in range(T):
                        ins = nc.tensor.matmul(
                            psG[:, jj % 2, :], wg[:, dbuf, t, :],
                            xmix[:, t, nch * 512:(nch + 1) * 512],
                            start=(t == 0), stop=(t == T - 1))
                    ins.then_inc(*cx.pe.tick())
                    mm_done[gt] = cx.pe.n
                    nc.scalar.wait_ge(cx.pe.h, cx.pe.n)
                    nc.scalar.activation(sg[:, jj % 2, :], psG[:, jj % 2, :],
                                         AF.Silu).then_inc(*cx.ac.tick())
                    nc.vector.wait_ge(cx.ac.h, cx.ac.n)
                    nc.vector.tensor_mul(
                        olnT[:, gt, nch * 512:(nch + 1) * 512],
                        olnT[:, gt, nch * 512:(nch + 1) * 512],
                        sg[:, jj % 2, :]).then_inc(*cx.ve.tick())
                    ev_done[jj] = cx.ve.n
            nc.vector.wait_ge(cx.ve.h, cx.ve.n)
            cx.barrier()

        dump("folnG", olnT[:])
        if stop_after == "g":
            return
        with ExitStack() as es:
            wo = es.enter_context(nc.sbuf_tensor(un("f_wo"), [128, 2, T, 128], BF16))
            xr = es.enter_context(nc.sbuf_tensor(un("f_xr"), [128, 2, 512], BF16))
            stg = es.enter_context(nc.sbuf_tensor(un("f_stgo"), [128, 2, 512], BF16))
            stgf = es.enter_context(nc.sbuf_tensor(un("f_stgf"), [128, 2, 512], F32))
            psO = es.enter_context(nc.psum_tensor(un("f_pso"), [128, 2, 512], F32))
            mm_done = {}
            ev_done = {}
            for dt in range(T):
                dbuf = dt % 2
                if dt >= 2:
                    nc.gpsimd.wait_ge(cx.pe.h, mm_done[dt - 2])
                nc.gpsimd.dma_start(
                    out=wo[:, dbuf, :, :],
                    in_=inp["f_wout"][:, :, dt, :].rearrange("t q e -> q t e")
                ).then_inc(*cx.gw.tick(16))
                nc.tensor.wait_ge(cx.gw.h, cx.gw.n)
                for nch in range(4):
                    jj = dt * 4 + nch
                    if jj >= 2:
                        nc.tensor.wait_ge(cx.ve.h, ev_done.get(jj - 2, 0))
                    for t in range(T):
                        ins = nc.tensor.matmul(
                            psO[:, jj % 2, :], wo[:, dbuf, t, :],
                            olnT[:, t, nch * 512:(nch + 1) * 512],
                            start=(t == 0), stop=(t == T - 1))
                    ins.then_inc(*cx.pe.tick())
                    mm_done[dt] = cx.pe.n
                    nc.sync.dma_start(
                        out=xr[:, jj % 2, :],
                        in_=xres[dt, :, nch * 512:(nch + 1) * 512]
                    ).then_inc(*cx.sd.tick(16))
                    nc.vector.wait_ge(cx.pe.h, cx.pe.n)
                    nc.vector.wait_ge(cx.sd.h, cx.sd.n)
                    nc.vector.tensor_add(stg[:, jj % 2, :], psO[:, jj % 2, :],
                                         xr[:, jj % 2, :])
                    if out_fp32 is not None:
                        nc.vector.tensor_add(stgf[:, jj % 2, :],
                                             psO[:, jj % 2, :], xr[:, jj % 2, :])
                    nc.vector.engine_nop().then_inc(*cx.ve.tick())
                    ev_done[jj] = cx.ve.n
                    nc.sync.wait_ge(cx.ve.h, cx.ve.n)
                    nc.sync.dma_start(
                        out=xres[dt, :, nch * 512:(nch + 1) * 512],
                        in_=stg[:, jj % 2, :]).then_inc(*cx.sd.tick(16))
                    if out_fp32 is not None:
                        nc.sync.dma_start(
                            out=out_fp32[dt, :, nch * 512:(nch + 1) * 512],
                            in_=stgf[:, jj % 2, :]).then_inc(*cx.sd.tick(16))
            nc.sync.wait_ge(cx.sd.h, cx.sd.n)
            cx.barrier()


def emit_mla_proj(cx, ln, inp, m_in):
    nc = cx.nc
    with ExitStack() as es:
        wqa = es.enter_context(nc.sbuf_tensor(un("m_wqa"), [128, T, 4, 128], BF16))
        wkva = es.enter_context(nc.sbuf_tensor(un("m_wkva"), [128, T, 576], BF16))
        stg = es.enter_context(nc.sbuf_tensor(un("m_stg"), [128, 2, 512], BF16))
        stgp = es.enter_context(nc.sbuf_tensor(un("m_stgp"), [64, 2, 512], BF16))
        kct = es.enter_context(nc.sbuf_tensor(un("m_kc"), [32, 2, 512], F32))
        psA = es.enter_context(nc.psum_tensor(un("m_ps"), [128, 2, 512], F32))
        psP = es.enter_context(nc.psum_tensor(un("m_psp"), [128, 512], F32))
        cosl = es.enter_context(nc.sbuf_tensor(un("m_cosl"), [32, SL], F32))
        sinl = es.enter_context(nc.sbuf_tensor(un("m_sinl"), [32, SL], F32))
        nc.gpsimd.dma_start(out=wqa[:], in_=inp["wqa"].rearrange(
            "t r q e -> q t r e")).then_inc(*cx.gw.tick(16))
        nc.gpsimd.dma_start(out=wkva[:], in_=inp["wkva"].rearrange(
            "t q e -> q t e")).then_inc(*cx.gw.tick(16))
        nc.gpsimd.dma_start(out=cosl[:], in_=inp["kpe_cos"]).then_inc(*cx.gw.tick(16))
        nc.gpsimd.dma_start(out=sinl[:], in_=inp["kpe_sin"]).then_inc(*cx.gw.tick(16))
        nc.tensor.wait_ge(cx.gw.h, cx.gw.n)
        nc.vector.wait_ge(cx.gw.h, cx.gw.n)
        done = {}
        for o in range(8):
            for nch in range(4):
                j = o * 4 + nch
                if j >= 2:
                    nc.tensor.wait_ge(cx.ac.h, done[j - 2])
                for t in range(T):
                    if o < 4:
                        lhsT = wqa[:, t, o, :]
                    else:
                        lhsT = wkva[:, t, (o - 4) * 128:(o - 3) * 128]
                    ins = nc.tensor.matmul(
                        psA[:, j % 2, :], lhsT,
                        ln[:, t, nch * 512:(nch + 1) * 512],
                        start=(t == 0), stop=(t == T - 1))
                ins.then_inc(*cx.pe.tick())
                nc.scalar.wait_ge(cx.pe.h, cx.pe.n)
                if j >= 2:
                    nc.scalar.wait_ge(cx.sd.h, cx.sd.n - 16)
                nc.scalar.activation(stg[:, j % 2, :], psA[:, j % 2, :],
                                     AF.Identity).then_inc(*cx.ac.tick())
                done[j] = cx.ac.n
                nc.sync.wait_ge(cx.ac.h, cx.ac.n)
                nc.sync.dma_start(
                    out=m_in[o * 128:(o + 1) * 128, nch * 512:(nch + 1) * 512],
                    in_=stg[:, j % 2, :]).then_inc(*cx.sd.tick(16))
        vdone = {}
        for nch in range(4):
            if nch >= 1:
                nc.tensor.wait_ge(cx.ve.h, vdone.get(nch - 1, 0))
            for t in range(T):
                ins = nc.tensor.matmul(
                    psP[0:64, :], wkva[:, t, 512:576],
                    ln[:, t, nch * 512:(nch + 1) * 512],
                    start=(t == 0), stop=(t == T - 1))
            ins.then_inc(*cx.pe.tick())
            v = VS(cx)
            v.wait_ge(cx.pe.h, cx.pe.n)
            if nch >= 2:
                v.wait_ge(cx.sd.h, cx.sd.n - 16)
            v.tensor_mul(kct[:, 0, :], psP[0:32, :], cosl[:])
            v.tensor_mul(kct[:, 1, :], psP[32:64, :], sinl[:])
            v.tensor_sub(kct[:, 0, :], kct[:, 0, :], kct[:, 1, :])
            v.tensor_copy(stgp[0:32, nch % 2, :], kct[:, 0, :])
            v.tensor_mul(kct[:, 0, :], psP[0:32, :], sinl[:])
            v.tensor_mul(kct[:, 1, :], psP[32:64, :], cosl[:])
            v.tensor_add(kct[:, 0, :], kct[:, 0, :], kct[:, 1, :])
            v.tensor_copy(stgp[32:64, nch % 2, :], kct[:, 0, :])
            vdone[nch] = cx.ve.n
            nc.sync.wait_ge(cx.ve.h, cx.ve.n)
            nc.sync.dma_start(
                out=m_in[1024:1088, nch * 512:(nch + 1) * 512],
                in_=stgp[:, nch % 2, :]).then_inc(*cx.sd.tick(16))
        nc.sync.wait_ge(cx.sd.h, cx.sd.n)
        cx.barrier()


def emit_mla_expand(cx, consts, inp, m_all, bq, kn, bv):
    nc = cx.nc
    with ExitStack() as es:
        wqb = es.enter_context(nc.sbuf_tensor(un("x_wqb"), [128, 4, 384], BF16))
        wkk = es.enter_context(nc.sbuf_tensor(un("x_wkk"), [128, 4, 256], BF16))
        wkv = es.enter_context(nc.sbuf_tensor(un("x_wkv"), [128, 4, 256], BF16))
        qa = es.enter_context(nc.sbuf_tensor(un("x_qa"), [128, 2, 4, 512], BF16))
        co = es.enter_context(nc.sbuf_tensor(un("x_co"), [128, 2, 4, 512], BF16))
        stg = es.enter_context(nc.sbuf_tensor(un("x_stg"), [128, 6, 512], BF16))
        rop = es.enter_context(nc.sbuf_tensor(un("x_rop"), [64, 4, 512], F32))
        psN = es.enter_context(nc.psum_tensor(un("x_psn"), [128, 2, 512], F32))
        psE = es.enter_context(nc.psum_tensor(un("x_pse"), [128, 512], F32))
        psK = es.enter_context(nc.psum_tensor(un("x_psk"), [128, 2, 512], F32))
        psV = es.enter_context(nc.psum_tensor(un("x_psv"), [128, 2, 512], F32))
        mcos = es.enter_context(nc.sbuf_tensor(un("x_mcos"), [64, S], F32))
        msin = es.enter_context(nc.sbuf_tensor(un("x_msin"), [64, S], F32))
        nc.gpsimd.dma_start(out=mcos[:], in_=inp["mla_cos2"]
                            ).then_inc(*cx.gw.tick(16))
        nc.gpsimd.dma_start(out=msin[:], in_=inp["mla_sin2"]
                            ).then_inc(*cx.gw.tick(16))
        me = nc.gpsimd.partition_id()
        nc.gpsimd.dma_start(
            out=wqb[:],
            in_=inp["wqb"].rearrange("c q r e -> (c q) r e")[
                bass.ds(me * 128, 128), :, :]).then_inc(*cx.gw.tick(16))
        nc.gpsimd.dma_start(
            out=wkk[:],
            in_=inp["wkvb_k"].rearrange("c q r e -> (c q) r e")[
                bass.ds(me * 128, 128), :, :]).then_inc(*cx.gw.tick(16))
        nc.gpsimd.dma_start(
            out=wkv[:],
            in_=inp["wkvb_v"].rearrange("c q r e -> (c q) r e")[
                bass.ds(me * 128, 128), :, :]).then_inc(*cx.gw.tick(16))
        ld = {}
        used = {}
        for ch in range(32):
            j, b_ = divmod(ch, 4)
            buf = ch % 2
            c0 = ch * 512
            if ch >= 2:
                nc.sync.wait_ge(cx.pe.h, used.get(ch - 2, 0))
            for rt in range(4):
                nc.sync.dma_start(
                    out=qa[:, buf, rt, :],
                    in_=m_all[j * 1088 + rt * 128:j * 1088 + (rt + 1) * 128,
                              b_ * 512:(b_ + 1) * 512]).then_inc(*cx.sd.tick(16))
                nc.sync.dma_start(
                    out=co[:, buf, rt, :],
                    in_=m_all[j * 1088 + 512 + rt * 128:
                              j * 1088 + 512 + (rt + 1) * 128,
                              b_ * 512:(b_ + 1) * 512]).then_inc(*cx.sd.tick(16))
            nc.tensor.wait_ge(cx.sd.h, cx.sd.n)
            if ch >= 1:
                nc.tensor.wait_ge(cx.ve.h, cx.ve.n)
            vcopy = {}
            for mi, (m0, msz, bank) in enumerate(
                    [(0, 128, 0), (128, 128, 1)]):
                for rt in range(4):
                    ins = nc.tensor.matmul(
                        psN[0:msz, mi, :], wqb[:, rt, m0:m0 + msz],
                        qa[:, buf, rt, :], start=(rt == 0), stop=(rt == 3))
                ins.then_inc(*cx.pe.tick())
            for rt in range(4):
                ins = nc.tensor.matmul(psE[0:64, :], wqb[:, rt, 256:320],
                                       qa[:, buf, rt, :],
                                       start=(rt == 0), stop=(rt == 3))
            for rt in range(4):
                ins = nc.tensor.matmul(psE[64:128, :], wqb[:, rt, 320:384],
                                       qa[:, buf, rt, :],
                                       start=(rt == 0), stop=(rt == 3))
            ins.then_inc(*cx.pe.tick())
            for mi in range(2):
                for rt in range(4):
                    ins = nc.tensor.matmul(
                        psK[:, mi, :], wkk[:, rt, mi * 128:(mi + 1) * 128],
                        co[:, buf, rt, :], start=(rt == 0), stop=(rt == 3))
                ins.then_inc(*cx.pe.tick())
            for pt_ in range(4):
                if pt_ >= 2:
                    nc.tensor.wait_ge(cx.ve.h, vcopy[pt_ - 2])
                for rt in range(4):
                    ins = nc.tensor.matmul(
                        psV[:, pt_ % 2, 0:256],
                        co[:, buf, rt, pt_ * 128:(pt_ + 1) * 128],
                        wkv[:, rt, :], start=(rt == 0), stop=(rt == 3))
                ins.then_inc(*cx.pe.tick())
                nc.vector.wait_ge(cx.pe.h, cx.pe.n)
                nc.vector.wait_ge(cx.sd.h, cx.sd.n)
                nc.vector.tensor_copy(stg[:, 4 + pt_ % 2, 0:256],
                                      psV[:, pt_ % 2, 0:256]
                                      ).then_inc(*cx.ve.tick())
                vcopy[pt_] = cx.ve.n
                nc.sync.wait_ge(cx.ve.h, cx.ve.n)
                nc.sync.dma_start(out=bv[c0 + pt_ * 128:c0 + (pt_ + 1) * 128,
                                         0:256],
                                  in_=stg[:, 4 + pt_ % 2, 0:256]
                                  ).then_inc(*cx.sd.tick(16))
            v = VS(cx)
            v.wait_ge(cx.pe.h, cx.pe.n)
            if ch >= 1:
                v.wait_ge(cx.sd.h, cx.sd.n)
            v.tensor_copy(stg[:, 0, :], psN[:, 0, :])
            v.tensor_copy(stg[:, 1, :], psN[:, 1, :])
            v.tensor_copy(stg[:, 2, :], psK[:, 0, :])
            v.tensor_copy(stg[:, 3, :], psK[:, 1, :])
            nc.vector.wait_ge(cx.gw.h, cx.gw.n)
            cos2 = mcos[:, j * 512:(j + 1) * 512]
            sin2 = msin[:, j * 512:(j + 1) * 512]
            v.tensor_mul(rop[:, 0, :], psE[0:64, :], cos2)
            v.tensor_mul(rop[:, 1, :], psE[64:128, :], sin2)
            v.tensor_sub(rop[:, 0, :], rop[:, 0, :], rop[:, 1, :])
            v.tensor_mul(rop[:, 1, :], psE[0:64, :], sin2)
            v.tensor_mul(rop[:, 2, :], psE[64:128, :], cos2)
            v.tensor_add(rop[:, 1, :], rop[:, 1, :], rop[:, 2, :])
            used[ch] = cx.pe.n
            nc.sync.wait_ge(cx.ve.h, cx.ve.n)
            nc.sync.dma_start(out=bq[0, 0:128, c0:c0 + 512],
                              in_=stg[:, 0, :]).then_inc(*cx.sd.tick(16))
            nc.sync.dma_start(out=bq[1, 0:128, c0:c0 + 512],
                              in_=stg[:, 1, :]).then_inc(*cx.sd.tick(16))
            nc.sync.dma_start(out=kn[0, :, c0:c0 + 512],
                              in_=stg[:, 2, :]).then_inc(*cx.sd.tick(16))
            nc.sync.dma_start(out=kn[1, :, c0:c0 + 512],
                              in_=stg[:, 3, :]).then_inc(*cx.sd.tick(16))
            nc.gpsimd.wait_ge(cx.ve.h, cx.ve.n)
            nc.gpsimd.dma_start(out=bq[0, 128:160, c0:c0 + 512],
                                in_=rop[0:32, 0, :]).then_inc(*cx.gw.tick(16))
            nc.gpsimd.dma_start(out=bq[1, 128:160, c0:c0 + 512],
                                in_=rop[32:64, 0, :]).then_inc(*cx.gw.tick(16))
            nc.gpsimd.dma_start(out=bq[0, 160:192, c0:c0 + 512],
                                in_=rop[0:32, 1, :]).then_inc(*cx.gw.tick(16))
            nc.gpsimd.dma_start(out=bq[1, 160:192, c0:c0 + 512],
                                in_=rop[32:64, 1, :]).then_inc(*cx.gw.tick(16))
        nc.sync.wait_ge(cx.sd.h, cx.sd.n)
        cx.barrier()


def emit_attn_inner_b(cx, consts, bq, kn, bv, m_all, bz_in):
    nc = cx.nc
    sc = 1.0 / math.sqrt(192.0)
    with ExitStack() as es:
        kb_ = es.enter_context(nc.sbuf_tensor(un("ab_k"), [128, 2, 32, 128], BF16))
        kpe = es.enter_context(nc.sbuf_tensor(un("ab_kpe"), [64, 2, 32, 128], BF16))
        qb_ = es.enter_context(nc.sbuf_tensor(un("ab_q"), [128, 2, 32, 128], BF16))
        qpe = es.enter_context(nc.sbuf_tensor(un("ab_qpe"), [64, 2, 32, 128], BF16))
        vb_ = es.enter_context(nc.sbuf_tensor(un("ab_v"), [128, 2, 32, 128], BF16))
        pt = es.enter_context(nc.sbuf_tensor(un("ab_p"), [128, 2, 512], BF16))
        sC = es.enter_context(nc.sbuf_tensor(un("ab_c"), [128, 1], F32))
        sml = es.enter_context(nc.sbuf_tensor(un("ab_sml"), [128, 4], F32))
        qss = es.enter_context(nc.sbuf_tensor(un("ab_qss"), [128, 4, 128], BF16))
        rcp = es.enter_context(nc.sbuf_tensor(un("ab_rcp"), [128, 512], F32))
        zt = es.enter_context(nc.sbuf_tensor(un("ab_z"), [128, 2, 512], BF16))
        psS = es.enter_context(nc.psum_tensor(un("ab_psS"), [128, 2, 512], F32))
        psO = es.enter_context(nc.psum_tensor(un("ab_psO"), [128, 512], F32))
        psB = es.enter_context(nc.psum_tensor(un("ab_psB"), [128, 512], F32))
        psR = es.enter_context(nc.psum_tensor(un("ab_psR"), [128, 128], F32))
        psN = es.enter_context(nc.psum_tensor(un("ab_psN"), [1, 512], F32))
        ones1 = consts["ones1"]
        onesr = consts["onesr_f"]
        diag = consts["diag"]
        pe_end = {}
        for inst in range(8):
            b, hp = divmod(inst, 2)
            buf = inst % 2
            if inst >= 2:
                nc.sync.wait_ge(cx.pe.h, pe_end[inst - 2])
            for j in range(NC):
                cs = j * 2048 + b * 512
                nc.sync.dma_start(
                    out=qb_[:, buf, j * 4:(j + 1) * 4, :].rearrange(
                        "p c e -> p (c e)"),
                    in_=bq[hp, 0:128, cs:cs + 512]).then_inc(*cx.sd.tick(16))
                nc.sync.dma_start(
                    out=qpe[:, buf, j * 4:(j + 1) * 4, :].rearrange(
                        "p c e -> p (c e)"),
                    in_=bq[hp, 128:192, cs:cs + 512]).then_inc(*cx.sd.tick(16))
                nc.sync.dma_start(
                    out=kb_[:, buf, j * 4:(j + 1) * 4, :].rearrange(
                        "p c e -> p (c e)"),
                    in_=kn[hp, :, cs:cs + 512]).then_inc(*cx.sd.tick(16))
                nc.sync.dma_start(
                    out=kpe[:, buf, j * 4:(j + 1) * 4, :].rearrange(
                        "p c e -> p (c e)"),
                    in_=m_all[j * 1088 + 1024:j * 1088 + 1088,
                              b * 512:(b + 1) * 512]).then_inc(*cx.sd.tick(16))
                for cc_ in range(4):
                    r0 = j * 2048 + b * 512 + cc_ * 128
                    nc.sync.dma_start(
                        out=vb_[:, buf, j * 4 + cc_, :],
                        in_=bv[r0:r0 + 128, hp * 128:(hp + 1) * 128]
                    ).then_inc(*cx.sd.tick(16))
            ld_n = cx.sd.n
            nc.vector.wait_ge(cx.sd.h, ld_n)
            if inst >= 1:
                nc.vector.wait_ge(cx.pe.h, cx.pe.n)
            nc.vector.tensor_copy(qss[:, 0, :], qb_[:, buf, :, 0:4])
            nc.vector.tensor_copy(qss[:, 1, :], kb_[:, buf, :, 0:4])
            nc.vector.tensor_copy(qss[0:64, 2, :], qpe[:, buf, :, 0:4])
            nc.vector.tensor_copy(qss[0:64, 3, :], kpe[:, buf, :, 0:4]
                                  ).then_inc(*cx.ve.tick())
            nc.tensor.wait_ge(cx.sd.h, ld_n)
            nc.tensor.wait_ge(cx.ve.h, cx.ve.n)
            nc.tensor.matmul(psR[:], qss[:, 0, :], qss[:, 1, :],
                             start=True, stop=False, skip_group_check=True)
            nc.tensor.matmul(psR[:], qss[0:64, 2, :], qss[0:64, 3, :],
                             start=False, stop=True, skip_group_check=True
                             ).then_inc(*cx.pe.tick())
            nc.vector.wait_ge(cx.pe.h, cx.pe.n)
            nc.vector.tensor_reduce(sml[:, 0:1], psR[:], AX.X, ALU.max
                                    ).then_inc(*cx.ve.tick())
            nc.tensor.wait_ge(cx.ve.h, cx.ve.n)
            nc.tensor.transpose(psR[0:1, :], sml[:, 0:1], consts["ident_f"][:]
                                ).then_inc(*cx.pe.tick())
            nc.vector.wait_ge(cx.pe.h, cx.pe.n)
            nc.vector.tensor_reduce(sml[0:1, 1:2], psR[0:1, :], AX.X, ALU.max)
            nc.vector.tensor_scalar(sml[0:1, 2:3], sml[0:1, 1:2], -sc, -2.0 * sc,
                                    ALU.mult, ALU.add).then_inc(*cx.ve.tick())
            nc.tensor.wait_ge(cx.ve.h, cx.ve.n)
            nc.tensor.matmul(psR[:, 0:1], onesr[:], sml[0:1, 2:3],
                             start=True, stop=True).then_inc(*cx.pe.tick())
            nc.vector.wait_ge(cx.pe.h, cx.pe.n)
            nc.vector.wait_ge(cx.ac.h, cx.ac.n)
            nc.vector.tensor_copy(sC[:], psR[:, 0:1]).then_inc(*cx.ve.tick())
            nc.scalar.wait_ge(cx.ve.h, cx.ve.n)
            for jq in range(8):
                nk = 4 * jq + 4
                for k in range(nk):
                    sb = k % 2
                    if k >= 2:
                        nc.tensor.wait_ge(cx.ac.h, cx.ac.n - 1)
                    nc.tensor.matmul(
                        psS[:, sb, :], kb_[:, buf, k, :],
                        qb_[:, buf, jq * 4:(jq + 1) * 4, :].rearrange(
                            "p c e -> p (c e)"),
                        start=True, stop=False, skip_group_check=True)
                    ins = nc.tensor.matmul(
                        psS[:, sb, :], kpe[:, buf, k, :],
                        qpe[:, buf, jq * 4:(jq + 1) * 4, :].rearrange(
                            "p c e -> p (c e)"),
                        start=False, stop=True, skip_group_check=True)
                    ins.then_inc(*cx.pe.tick())
                    v_of_k = k - 4 * jq
                    if 0 <= v_of_k < 4:
                        nc.vector.wait_ge(cx.pe.h, cx.pe.n)
                        nc.vector.tensor_tensor(
                            psS[:, sb, :], psS[:, sb, :], diag[:, v_of_k, :],
                            ALU.add).then_inc(*cx.ve.tick())
                        nc.scalar.wait_ge(cx.ve.h, cx.ve.n)
                    else:
                        nc.scalar.wait_ge(cx.pe.h, cx.pe.n)
                    if k >= 2:
                        nc.scalar.wait_ge(cx.pe.h, cx.pe.n)
                    nc.scalar.activation(pt[:, sb, :], psS[:, sb, :], AF.Exp,
                                         bias=sC[:, 0:1], scale=sc
                                         ).then_inc(*cx.ac.tick())
                    nc.tensor.wait_ge(cx.ac.h, cx.ac.n)
                    nc.tensor.matmul(psO[:], vb_[:, buf, k, :], pt[:, sb, :],
                                     start=(k == 0), stop=(k == nk - 1),
                                     skip_group_check=True)
                    nc.tensor.matmul(psN[:], ones1[:], pt[:, sb, :],
                                     start=(k == 0), stop=(k == nk - 1),
                                     skip_group_check=True).then_inc(*cx.pe.tick())
                nc.vector.wait_ge(cx.pe.h, cx.pe.n)
                nc.vector.reciprocal(rcp[0:1, :], psN[:]).then_inc(*cx.ve.tick())
                nc.tensor.wait_ge(cx.ve.h, cx.ve.n)
                nc.tensor.matmul(psB[:], onesr[:], rcp[0:1, :],
                                 start=True, stop=True).then_inc(*cx.pe.tick())
                nc.vector.wait_ge(cx.pe.h, cx.pe.n)
                if jq >= 2:
                    nc.vector.wait_ge(cx.sd.h, cx.sd.n)
                nc.vector.tensor_copy(rcp[:], psB[:])
                nc.vector.tensor_mul(zt[:, jq % 2, :], psO[:], rcp[:]
                                     ).then_inc(*cx.ve.tick())
                nc.sync.wait_ge(cx.ve.h, cx.ve.n)
                nc.sync.dma_start(
                    out=bz_in[hp * 128:(hp + 1) * 128,
                              b * 4096 + jq * 512:b * 4096 + (jq + 1) * 512],
                    in_=zt[:, jq % 2, :]).then_inc(*cx.sd.tick(16))
                nc.tensor.wait_ge(cx.ve.h, cx.ve.n)
            pe_end[inst] = cx.pe.n
        nc.sync.wait_ge(cx.sd.h, cx.sd.n)
        cx.barrier()


def stage_inputs(inputs):
    x = inputs["L0_attn_a_x"]
    perm64 = np.concatenate([np.arange(0, 64, 2), np.arange(1, 64, 2)])

    common = {}
    common["wq_a"] = np.ascontiguousarray(
        inputs["L0_attn_a_W_Q"].reshape(H, T, 128, 128))
    common["wk_a"] = np.ascontiguousarray(
        inputs["L0_attn_a_W_K"].reshape(H, T, 128, 128))
    common["wv_a"] = np.ascontiguousarray(
        inputs["L0_attn_a_W_V"].transpose(1, 0, 2).reshape(T, 128, D))
    common["wo_a"] = np.ascontiguousarray(
        inputs["L0_attn_a_W_O"].reshape(H, 128, T, 128).transpose(0, 2, 1, 3))
    common["bq_a"] = np.ascontiguousarray(inputs["L0_attn_a_b_Q"].T)
    common["bk_a"] = np.ascontiguousarray(inputs["L0_attn_a_b_K"].T)
    common["bv_a"] = np.ascontiguousarray(
        np.broadcast_to(inputs["L0_attn_a_b_V"].reshape(1, D), (128, D)))
    common["bo_a"] = np.ascontiguousarray(
        inputs["L0_attn_a_b_O"].reshape(T, 128).T)

    common["wqa"] = np.ascontiguousarray(
        inputs["L0_attn_b_Wqa"].reshape(T, 128, 4, 128).transpose(0, 2, 1, 3))
    wqb = inputs["L0_attn_b_Wqb"].reshape(512, H, MLA_QH)
    qb = np.empty((NC, 512, 384), np.float32)
    for pr in range(NC):
        h0, h1 = 2 * pr, 2 * pr + 1
        qb[pr, :, 0:128] = wqb[:, h0, :128]
        qb[pr, :, 128:256] = wqb[:, h1, :128]
        qb[pr, :, 256:288] = wqb[:, h0, 128:][:, perm64[:32]]
        qb[pr, :, 288:320] = wqb[:, h1, 128:][:, perm64[:32]]
        qb[pr, :, 320:352] = wqb[:, h0, 128:][:, perm64[32:]]
        qb[pr, :, 352:384] = wqb[:, h1, 128:][:, perm64[32:]]
    common["wqb"] = np.ascontiguousarray(
        qb.reshape(NC, 4, 128, 384).transpose(0, 2, 1, 3))
    wkva = inputs["L0_attn_b_Wkva"].copy()
    wkva[:, 512:] = wkva[:, 512:][:, perm64]
    common["wkva"] = np.ascontiguousarray(wkva.reshape(T, 128, 576))
    wkvb = inputs["L0_attn_b_Wkvb"].reshape(512, H, 256)
    kb = np.empty((NC, 512, 256), np.float32)
    vb = np.empty((NC, 512, 256), np.float32)
    for pr in range(NC):
        h0, h1 = 2 * pr, 2 * pr + 1
        kb[pr, :, 0:128] = wkvb[:, h0, :128]
        kb[pr, :, 128:256] = wkvb[:, h1, :128]
        vb[pr, :, 0:128] = wkvb[:, h0, 128:]
        vb[pr, :, 128:256] = wkvb[:, h1, 128:]
    common["wkvb_k"] = np.ascontiguousarray(
        kb.reshape(NC, 4, 128, 256).transpose(0, 2, 1, 3))
    common["wkvb_v"] = np.ascontiguousarray(
        vb.reshape(NC, 4, 128, 256).transpose(0, 2, 1, 3))
    common["wout_b"] = np.ascontiguousarray(
        inputs["L0_attn_b_Wout"].reshape(H, 128, T, 128).transpose(0, 2, 1, 3))

    common["f_conv"] = np.ascontiguousarray(
        inputs["L0_ffn_conv_w"].reshape(T, 128, 2))
    common["f_wq"] = np.ascontiguousarray(
        inputs["L0_ffn_Wq"].reshape(T, 128, FDK))
    common["f_wk"] = np.ascontiguousarray(
        inputs["L0_ffn_Wk"].reshape(T, 128, FDK))
    common["f_wv"] = np.ascontiguousarray(
        inputs["L0_ffn_Wv"].reshape(T, 128, D))
    common["f_wg"] = np.ascontiguousarray(
        inputs["L0_ffn_Wg"].reshape(T, 128, T, 128))
    common["f_wout"] = np.ascontiguousarray(
        inputs["L0_ffn_Wout"].reshape(T, 128, T, 128))
    common["f_rw"] = np.ascontiguousarray(inputs["L0_ffn_router_w"])

    common["c_ident"] = np.eye(128, dtype=np.float32)
    theta = 10000.0 ** (-2.0 * np.arange(32, dtype=np.float64) / 64.0)
    ang = np.arange(S, dtype=np.float64)[:, None] * theta[None, :]
    cosT = np.cos(ang).astype(np.float32).T
    sinT = np.sin(ang).astype(np.float32).T
    common["mla_cos2"] = np.ascontiguousarray(np.concatenate([cosT, cosT], 0))
    common["mla_sin2"] = np.ascontiguousarray(np.concatenate([sinT, sinT], 0))
    inv_freq = 1.0 / 10000.0 ** (np.arange(0, FHDK, 2, dtype=np.float64) / FHDK)
    fang = np.arange(B, dtype=np.float64)[:, None] * inv_freq[None, :]
    fc = np.cos(fang).astype(np.float32)
    fs = np.sin(fang).astype(np.float32)
    common["ffn_cos"] = np.ascontiguousarray(
        np.broadcast_to(np.tile(fc, (1, H))[:, None, :], (B, 128, H * 32)))
    common["ffn_sin"] = np.ascontiguousarray(
        np.broadcast_to(np.tile(fs, (1, H))[:, None, :], (B, 128, H * 32)))
    kk = np.arange(128)[:, None]
    qq = np.arange(512)[None, :]
    mvar = np.stack([np.where(kk + 128 * v <= qq, 0.0, NEGBIG)
                     for v in range(4)]).astype(np.float32)
    common["c_diag"] = np.ascontiguousarray(mvar)

    in_maps = []
    for c in range(NC):
        m = dict(common)
        xs = x[:, c * SL:(c + 1) * SL, :]
        m["x_in"] = np.ascontiguousarray(
            xs.transpose(2, 0, 1).reshape(T, 128, RW))
        sl = slice(c * SL, (c + 1) * SL)
        m["kpe_cos"] = np.ascontiguousarray(cosT[:, sl])
        m["kpe_sin"] = np.ascontiguousarray(sinT[:, sl])
        in_maps.append(m)
    return in_maps


def declare_inputs(nc):
    d = {}
    d["x_in"] = dram(nc, "x_in", [T, 128, RW], F32, "ExternalInput")
    for nm, shape in [
        ("wq_a", [H, T, 128, 128]), ("wk_a", [H, T, 128, 128]),
        ("wv_a", [T, 128, D]), ("wo_a", [H, T, 128, 128]),
        ("bq_a", [128, H]), ("bk_a", [128, H]), ("bv_a", [128, D]),
        ("bo_a", [128, T]),
        ("wqa", [T, 4, 128, 128]), ("wqb", [NC, 128, 4, 384]),
        ("wkva", [T, 128, 576]), ("wkvb_k", [NC, 128, 4, 256]),
        ("wkvb_v", [NC, 128, 4, 256]), ("wout_b", [H, T, 128, 128]),
        ("f_conv", [T, 128, 2]), ("f_wq", [T, 128, FDK]), ("f_wk", [T, 128, FDK]),
        ("f_wv", [T, 128, D]), ("f_wg", [T, 128, T, 128]),
        ("f_wout", [T, 128, T, 128]), ("f_rw", [H, 128, 2]),
        ("c_ident", [128, 128]),
        ("mla_cos2", [64, S]), ("mla_sin2", [64, S]),
        ("ffn_cos", [B, 128, H * 32]), ("ffn_sin", [B, 128, H * 32]),
        ("c_diag", [4, 128, 512]),
        ("kpe_cos", [32, SL]), ("kpe_sin", [32, SL]),
    ]:
        d[nm] = dram(nc, nm, shape, F32, "ExternalInput")
    return d


def load_consts(cx, inp):
    nc = cx.nc
    c = {}
    c["ones1"] = cx.sb("c_ones1", [128, 1], BF16)
    c["onesr"] = cx.sb("c_onesr", [1, 128], BF16)
    c["onesr_f"] = cx.sb("c_onesr_f", [1, 128], F32)
    c["ident"] = cx.sb("c_identb", [128, 128], BF16)
    c["ident_f"] = cx.sb("c_identf", [128, 128], F32)
    c["diag"] = cx.sb("c_diag_s", [128, 4, 512], F32)
    nc.vector.memset(c["ones1"][:], 1.0)
    nc.vector.memset(c["onesr"][:], 1.0)
    nc.vector.memset(c["onesr_f"][:], 1.0)
    nc.gpsimd.dma_start(out=c["ident"][:], in_=inp["c_ident"]
                        ).then_inc(*cx.gw.tick(16))
    nc.gpsimd.dma_start(out=c["ident_f"][:], in_=inp["c_ident"]
                        ).then_inc(*cx.gw.tick(16))
    nc.gpsimd.dma_start(
        out=c["diag"][:],
        in_=inp["c_diag"].rearrange("v k q -> k v q")).then_inc(*cx.gw.tick(16))
    nc.gpsimd.wait_ge(cx.gw.h, cx.gw.n)
    return c


def build(taps=()):
    nc = bass.Bass("TRN2", target_bir_lowering=False)
    inp = declare_inputs(nc)
    out = dram(nc, "out", [T, 128, RW], F32, "ExternalOutput")
    tap_outs = {t: dram(nc, "tap_" + t, [T, 128, RW], F32, "ExternalOutput")
                for t in taps if t.startswith("x")}
    ln_taps = {t: dram(nc, "tap_" + t, [T, 128, RW], F32, "ExternalOutput")
               for t in taps if t.startswith("ln")}
    misc_taps = {}
    misc_outs = {}

    xres = dram(nc, "xres", [T, 128, RW], BF16)
    aq_in = dram(nc, "aq_in", [D, RW], BF16)
    ak_in = dram(nc, "ak_in", [D, RW], BF16)
    av_in = dram(nc, "av_in", [RW, D], BF16)
    aq_all = dram(nc, "aq_all", [NC * D, RW], BF16)
    ak_all = dram(nc, "ak_all", [NC * D, RW], BF16)
    av_all = dram(nc, "av_all", [NC * RW, D], BF16)
    az_in = dram(nc, "az_in", [HPC * 128, B * S], BF16)
    az_all = dram(nc, "az_all", [D, B * S], BF16)
    aq_my = dram(nc, "aq_my", [NC, HPC * 128, RW], BF16)
    ak_my = dram(nc, "ak_my", [NC, HPC * 128, RW], BF16)
    av_my = dram(nc, "av_my", [NC, RW, HPC * 128], BF16)
    az_my = dram(nc, "az_my", [D, B, 512], BF16)
    vffn = dram(nc, "vffn", [RW, D], BF16)
    m_in = dram(nc, "m_in", [1088, RW], BF16)
    m_all = dram(nc, "m_all", [NC * 1088, RW], BF16)
    bq = dram(nc, "bq", [HPC, 192, B * S], BF16)
    kn = dram(nc, "kn", [HPC, 128, B * S], BF16)
    bv = dram(nc, "bv", [B * S, HPC * 128], BF16)
    bz_in = dram(nc, "bz_in", [HPC * 128, B * S], BF16)
    bz_all = dram(nc, "bz_all", [D, B * S], BF16)
    bz_my = dram(nc, "bz_my", [D, B, 512], BF16)
    rg = [list(range(NC))]
    if "aqkvz" in taps:
        misc_taps["aq"] = (aq_in, [D, RW])
        misc_taps["ak"] = (ak_in, [D, RW])
        misc_taps["av"] = (av_in, [RW, D])
        misc_taps["az"] = (az_in, [HPC * 128, B * S])
        for nm, (srcap, shp) in misc_taps.items():
            misc_outs[nm] = dram(nc, "tapm_" + nm, shp, F32, "ExternalOutput")

    def extract_rows(dst, src_ap, blk, width):
        me = nc.sync.partition_id()
        view = src_ap.rearrange("(j d) r -> j d r", j=NC)
        nc.sync.dma_start(out=dst[:],
                          in_=view[:, bass.ds(me * width, width), :]
                          ).then_inc(*cx.sd.tick(16))

    def extract_cols(dst, src_ap, width):
        me = nc.sync.partition_id()
        view = src_ap.rearrange("(j d) r -> j d r", j=NC)
        nc.sync.dma_start(out=dst[:],
                          in_=view[:, :, bass.ds(me * width, width)]
                          ).then_inc(*cx.sd.tick(16))

    def extract_z(dst, src_ap):
        me = nc.sync.partition_id()
        view = src_ap.rearrange("d (b p) -> d b p", b=B)
        nc.sync.dma_start(out=dst[:],
                          in_=view[:, :, bass.ds(me * 512, 512)]
                          ).then_inc(*cx.sd.tick(16))

    def tap_x(name):
        if name in tap_outs:
            nc.gpsimd.dma_start(out=tap_outs[name], in_=xres
                                ).then_inc(*cx.gw.tick(16))
            nc.gpsimd.wait_ge(cx.gw.h, cx.gw.n)
            cx.barrier()

    def tap_dram(name, src):
        if name in misc_taps:
            nc.gpsimd.dma_start(out=misc_taps[name], in_=src
                                ).then_inc(*cx.gw.tick(16))
            nc.gpsimd.wait_ge(cx.gw.h, cx.gw.n)
            cx.barrier()

    cx = Ctx(nc)
    with cx.es:
        consts = load_consts(cx, inp)
        nc.gpsimd.dma_start(out=xres, in_=inp["x_in"]).then_inc(*cx.gw.tick(16))
        nc.gpsimd.wait_ge(cx.gw.h, cx.gw.n)
        cx.barrier()

        with ExitStack() as st_es:
            st = {"ln": st_es.enter_context(
                nc.sbuf_tensor(un("ln_a"), [128, T, RW], BF16))}
            emit_ln(cx, st, xres, consts)
            cx.barrier()
            if "ln1" in ln_taps:
                for t in range(T):
                    nc.gpsimd.dma_start(out=ln_taps["ln1"][t, :, :],
                                        in_=st["ln"][:, t, :]
                                        ).then_inc(*cx.gw.tick(16))
                nc.gpsimd.wait_ge(cx.gw.h, cx.gw.n)
                cx.barrier()
            emit_qkv_a(cx, st["ln"], inp, aq_in, ak_in, av_in)
        nc.gpsimd.collective_compute("AllGather", ALU.bypass, replica_groups=rg,
                                     ins=[aq_in.opt()], outs=[aq_all.opt()]
                                     ).then_inc(*cx.cc.tick())
        nc.gpsimd.collective_compute("AllGather", ALU.bypass, replica_groups=rg,
                                     ins=[ak_in.opt()], outs=[ak_all.opt()]
                                     ).then_inc(*cx.cc.tick())
        nc.gpsimd.collective_compute("AllGather", ALU.bypass, replica_groups=rg,
                                     ins=[av_in.opt()], outs=[av_all.opt()]
                                     ).then_inc(*cx.cc.tick())
        nc.gpsimd.wait_ge(cx.cc.h, cx.cc.n)
        cx.barrier()
        extract_rows(aq_my, aq_all, D, HPC * 128)
        extract_rows(ak_my, ak_all, D, HPC * 128)
        extract_cols(av_my, av_all, HPC * 128)
        nc.sync.wait_ge(cx.sd.h, cx.sd.n)
        cx.barrier()
        emit_attn_inner_a(cx, consts, aq_my, ak_my, av_my, az_in)
        nc.gpsimd.collective_compute("AllGather", ALU.bypass, replica_groups=rg,
                                     ins=[az_in.opt()], outs=[az_all.opt()]
                                     ).then_inc(*cx.cc.tick())
        nc.gpsimd.wait_ge(cx.cc.h, cx.cc.n)
        cx.barrier()
        extract_z(az_my, az_all)
        nc.sync.wait_ge(cx.sd.h, cx.sd.n)
        cx.barrier()
        if misc_taps:
            for nm, (srcap, shp) in misc_taps.items():
                nc.gpsimd.dma_start(out=misc_outs[nm], in_=srcap
                                    ).then_inc(*cx.gw.tick(16))
            nc.gpsimd.wait_ge(cx.gw.h, cx.gw.n)
            cx.barrier()
        emit_outproj(cx, inp["wo_a"], inp["bo_a"], az_my, xres)
        tap_x("x1")

        fdbg = {}
        if "ffndbg" in taps:
            for nm, shp in [("fq", [128, T, FDK]), ("fk", [128, T, FDK]),
                            ("fi0", [128, 16, H]), ("fal", [128, 16, H]),
                            ("fi0b", [128, 16, H]), ("fi0c", [128, 16, H]),
                            ("falc", [128, 16, H]),
                            ("fcf", [128, 4, 4, 4, H]), ("ftd", [128, 16, H]),
                            ("folnT", [128, T, RW]), ("folnG", [128, T, RW])]:
                fdbg[nm] = dram(nc, "tapf_" + nm, shp, F32, "ExternalOutput")
            misc_taps["fv"] = (vffn, [RW, D])
            misc_outs["fv"] = dram(nc, "tapm_fv", [RW, D], F32, "ExternalOutput")
        emit_ffn(cx, consts, inp, xres, vffn, dbg=fdbg)
        if "ffndbg" in taps:
            nc.gpsimd.dma_start(out=misc_outs["fv"], in_=vffn
                                ).then_inc(*cx.gw.tick(16))
            nc.gpsimd.wait_ge(cx.gw.h, cx.gw.n)
            cx.barrier()
        tap_x("x2")

        with ExitStack() as st_es:
            st = {"ln": st_es.enter_context(
                nc.sbuf_tensor(un("ln_b"), [128, T, RW], BF16))}
            emit_ln(cx, st, xres, consts)
            cx.barrier()
            if "ln3" in ln_taps:
                for t in range(T):
                    nc.gpsimd.dma_start(out=ln_taps["ln3"][t, :, :],
                                        in_=st["ln"][:, t, :]
                                        ).then_inc(*cx.gw.tick(16))
                nc.gpsimd.wait_ge(cx.gw.h, cx.gw.n)
                cx.barrier()
            emit_mla_proj(cx, st["ln"], inp, m_in)
        nc.gpsimd.collective_compute("AllGather", ALU.bypass, replica_groups=rg,
                                     ins=[m_in.opt()], outs=[m_all.opt()]
                                     ).then_inc(*cx.cc.tick())
        nc.gpsimd.wait_ge(cx.cc.h, cx.cc.n)
        cx.barrier()
        emit_mla_expand(cx, consts, inp, m_all, bq, kn, bv)
        emit_attn_inner_b(cx, consts, bq, kn, bv, m_all, bz_in)
        nc.gpsimd.collective_compute("AllGather", ALU.bypass, replica_groups=rg,
                                     ins=[bz_in.opt()], outs=[bz_all.opt()]
                                     ).then_inc(*cx.cc.tick())
        nc.gpsimd.wait_ge(cx.cc.h, cx.cc.n)
        cx.barrier()
        extract_z(bz_my, bz_all)
        nc.sync.wait_ge(cx.sd.h, cx.sd.n)
        cx.barrier()
        emit_outproj(cx, inp["wout_b"], None, bz_my, xres)
        tap_x("x3")

        emit_ffn(cx, consts, inp, xres, vffn, out_fp32=out)
        tap_x("x4")
    return nc


def kernel(**inputs):
    in_maps = stage_inputs(inputs)
    nc = build()
    res = run_bass_kernel_spmd(nc, in_maps, core_ids=list(range(NC)))
    return assemble(res, "out")


def assemble(res, name):
    parts = []
    for c in range(NC):
        a = res.results[c][name]
        parts.append(a.reshape(D, B, SL).transpose(1, 2, 0))
    return np.concatenate(parts, axis=1)


# version 02 (final)
# speedup vs baseline: 1.2218x; 1.2218x over previous
import math
from contextlib import ExitStack

import numpy as np
import concourse.bass as bass
import concourse.mybir as mybir
from concourse.bass_utils import run_bass_kernel_spmd

F32 = mybir.dt.float32
BF16 = mybir.dt.bfloat16
AF = mybir.ActivationFunctionType
ALU = mybir.AluOpType
AX = mybir.AxisListType

NC = 8
B = 4
S = 4096
D = 2048
T = D // 128
SL = S // NC
RW = B * SL
H = 16
HPC = H // NC
EPS = 1e-5
NEGBIG = -3.0e10

MLA_NOPE = 128
MLA_ROPE = 64
MLA_QH = 192
MLA_RANK = 512
FDK = 1024
FHDK = 64

DEBUG_TAPS = []


class SemCtr:
    def __init__(self, nc, name):
        self._cm = nc.semaphore(name)
        self.h = self._cm.__enter__()
        self.n = 0

    def inc(self, k=1):
        self.n += k
        return self.h

    def tick(self, k=1):
        self.n += k
        return (self.h, k)


class VS:

    def __init__(self, cx):
        self._cx = cx

    def __getattr__(self, name):
        cx = self._cx
        f = getattr(cx.nc.vector, name)
        if name in ("wait_ge", "partition_id"):
            return f

        def g(*a, **k):
            ins = f(*a, **k)
            ins.then_inc(*cx.ve.tick())
            cx.nc.vector.wait_ge(cx.ve.h, cx.ve.n)
            return ins
        return g


class Ctx:

    def __init__(self, nc):
        self.nc = nc
        self.es = ExitStack()
        self.gw = SemCtr(nc, "gw")
        self.sd = SemCtr(nc, "sd")
        self.ad = SemCtr(nc, "ad")
        self.cc = SemCtr(nc, "cc")
        self.pe = SemCtr(nc, "pe")
        self.ve = SemCtr(nc, "ve")
        self.ac = SemCtr(nc, "ac")
        self.ge = SemCtr(nc, "ge")

    def barrier(self):
        self.nc.all_engine_barrier()

    def sb(self, name, shape, dtype):
        return self.es.enter_context(self.nc.sbuf_tensor(name, shape, dtype))

    def ps(self, name, shape, dtype=F32):
        return self.es.enter_context(self.nc.psum_tensor(name, shape, dtype))


_UNIQ = [0]


def un(nm):
    _UNIQ[0] += 1
    return f"{nm}_{_UNIQ[0]}"


def dram(nc, name, shape, dtype, kind="Internal"):
    return nc.dram_tensor(name, shape, dtype, kind=kind).ap()



def emit_ln(cx, st, xres, consts, pad_rows=0):
    nc = cx.nc
    ln = st["ln"]
    ones1 = consts["ones1"]
    onesr = consts["onesr_f"]
    CH = 512
    NCH = RW // CH
    with ExitStack() as es:
        xt = es.enter_context(nc.sbuf_tensor(un("ln_xt"), [128, T, CH], BF16))
        sq = es.enter_context(nc.sbuf_tensor(un("ln_sq"), [128, 2, CH], BF16))
        stats = es.enter_context(nc.sbuf_tensor(un("ln_stats"), [1, 6, CH], F32))
        p_sum = es.enter_context(nc.psum_tensor(un("ln_psum"), [1, 2, CH], F32))
        p_bc = es.enter_context(nc.psum_tensor(un("ln_bc"), [128, 2, CH], F32))
        tmf = es.enter_context(nc.sbuf_tensor(un("ln_tmf"), [128, CH], F32))
        if pad_rows:
            cx.nc.vector.memset(ln[:, :, 0:pad_rows], 0.0).then_inc(*cx.ve.tick())
        for c in range(NCH):
            r0 = c * CH
            for t in range(T):
                nc.sync.dma_start(out=xt[:, t, :], in_=xres[t, :, r0:r0 + CH]
                                  ).then_inc(*cx.sd.tick(16))
            nc.tensor.wait_ge(cx.sd.h, cx.sd.n)
            for t in range(T):
                nc.tensor.matmul(p_sum[:, 0, :], ones1[:], xt[:, t, :],
                                 start=(t == 0), stop=(t == T - 1))
            nc.scalar.wait_ge(cx.sd.h, cx.sd.n)
            pe_base = cx.pe.n
            for t in range(T):
                if t >= 2:
                    nc.scalar.wait_ge(cx.pe.h, pe_base + t - 1)
                nc.scalar.activation(sq[:, t % 2, :], xt[:, t, :], AF.Square
                                     ).then_inc(*cx.ac.tick())
                nc.tensor.wait_ge(cx.ac.h, cx.ac.n)
                nc.tensor.matmul(p_sum[:, 1, :], ones1[:], sq[:, t % 2, :],
                                 start=(t == 0), stop=(t == T - 1)
                                 ).then_inc(*cx.pe.tick())
            v = nc.vector
            v.wait_ge(cx.pe.h, cx.pe.n)
            v.tensor_scalar_mul(stats[:, 0, :], p_sum[:, 0, :], 1.0 / D)
            v.tensor_scalar_mul(stats[:, 1, :], p_sum[:, 1, :], 1.0 / D)
            v.tensor_mul(stats[:, 2, :], stats[:, 0, :], stats[:, 0, :])
            v.tensor_sub(stats[:, 2, :], stats[:, 1, :], stats[:, 2, :])
            v.tensor_scalar_add(stats[:, 2, :], stats[:, 2, :], EPS
                                ).then_inc(*cx.ve.tick())
            nc.scalar.wait_ge(cx.ve.h, cx.ve.n)
            nc.scalar.activation(stats[:, 4, :], stats[:, 2, :], AF.Sqrt
                                 ).then_inc(*cx.ac.tick())
            v.wait_ge(cx.ac.h, cx.ac.n)
            v.reciprocal(stats[:, 3, :], stats[:, 4, :])
            v.tensor_mul(stats[:, 4, :], stats[:, 3, :], stats[:, 3, :])
            v.tensor_mul(stats[:, 4, :], stats[:, 4, :], stats[:, 2, :])
            v.tensor_scalar(stats[:, 4, :], stats[:, 4, :], -0.5, 1.5,
                            ALU.mult, ALU.add)
            v.tensor_mul(stats[:, 3, :], stats[:, 3, :], stats[:, 4, :])
            v.tensor_scalar_mul(stats[:, 0, :], stats[:, 0, :], -1.0
                                ).then_inc(*cx.ve.tick())
            nc.tensor.wait_ge(cx.ve.h, cx.ve.n)
            nc.tensor.matmul(p_bc[:, 0, :], onesr[:], stats[:, 0, :],
                             start=True, stop=True)
            nc.tensor.matmul(p_bc[:, 1, :], onesr[:], stats[:, 3, :],
                             start=True, stop=True).then_inc(*cx.pe.tick())
            v.wait_ge(cx.pe.h, cx.pe.n)
            for t in range(T):
                v.tensor_add(tmf[:], xt[:, t, :], p_bc[:, 0, :])
                ins = v.tensor_mul(ln[:, t, pad_rows + r0:pad_rows + r0 + CH],
                                   tmf[:], p_bc[:, 1, :])
            ins.then_inc(*cx.ve.tick())
            nc.sync.wait_ge(cx.ve.h, cx.ve.n)
            nc.tensor.wait_ge(cx.ve.h, cx.ve.n)


def emit_qkv_a(cx, ln, inp, aq_in, ak_in, av_in):
    nc = cx.nc
    with ExitStack() as es:
        wb = es.enter_context(nc.sbuf_tensor(un("qa_w"), [128, 2, T, 128], BF16))
        bias = es.enter_context(nc.sbuf_tensor(un("qa_b"), [128, 2 * H], F32))
        stg = es.enter_context(nc.sbuf_tensor(un("qa_stg"), [128, 2, 512], BF16))
        psA = es.enter_context(nc.psum_tensor(un("qa_ps"), [128, 2, 512], F32))
        nc.gpsimd.dma_start(out=bias[:, 0:H], in_=inp["bq_a"]
                            ).then_inc(*cx.gw.tick(16))
        nc.gpsimd.dma_start(out=bias[:, H:2 * H], in_=inp["bk_a"]
                            ).then_inc(*cx.gw.tick(16))
        mm_done = {}
        evac_done = {}
        for i in range(2 * H):
            kind, h = divmod(i, H)
            wsrc = inp["wq_a"] if kind == 0 else inp["wk_a"]
            dst = aq_in if kind == 0 else ak_in
            buf = i % 2
            if i >= 2:
                nc.gpsimd.wait_ge(cx.pe.h, mm_done[i - 2])
            nc.gpsimd.dma_start(
                out=wb[:, buf, :, :],
                in_=wsrc[h].rearrange("t q e -> q t e")).then_inc(*cx.gw.tick(16))
            nc.tensor.wait_ge(cx.gw.h, cx.gw.n)
            for nch in range(4):
                if i * 4 + nch >= 2:
                    nc.tensor.wait_ge(cx.ac.h, evac_done.get(i * 4 + nch - 2, 0))
                for t in range(T):
                    ins = nc.tensor.matmul(
                        psA[:, nch % 2, :], wb[:, buf, t, :],
                        ln[:, t, nch * 512:(nch + 1) * 512],
                        start=(t == 0), stop=(t == T - 1))
                ins.then_inc(*cx.pe.tick())
                nc.scalar.wait_ge(cx.pe.h, cx.pe.n)
                nc.scalar.activation(
                    stg[:, nch % 2, :], psA[:, nch % 2, :], AF.Identity,
                    bias=bias[:, i:i + 1], scale=1.0).then_inc(*cx.ac.tick())
                evac_done[i * 4 + nch] = cx.ac.n
                nc.sync.wait_ge(cx.ac.h, cx.ac.n)
                nc.sync.dma_start(
                    out=dst[h * 128:(h + 1) * 128, nch * 512:(nch + 1) * 512],
                    in_=stg[:, nch % 2, :]).then_inc(*cx.sd.tick(16))
                nc.scalar.wait_ge(cx.sd.h, cx.sd.n - 16)
            mm_done[i] = cx.pe.n
        nc.sync.wait_ge(cx.sd.h, cx.sd.n)
        cx.barrier()

    with ExitStack() as es:
        wv = es.enter_context(nc.sbuf_tensor(un("qa_wv"), [128, T, D], BF16))
        bv = es.enter_context(nc.sbuf_tensor(un("qa_bv"), [128, D], F32))
        stg = es.enter_context(nc.sbuf_tensor(un("qa_stgv"), [128, 2, 512], BF16))
        psV = es.enter_context(nc.psum_tensor(un("qa_psv"), [128, 4, 512], F32))
        for t in range(T):
            nc.gpsimd.dma_start(out=wv[:, t, :], in_=inp["wv_a"][t]
                                ).then_inc(*cx.gw.tick(16))
        nc.gpsimd.dma_start(out=bv[:], in_=inp["bv_a"]).then_inc(*cx.gw.tick(16))
        nc.tensor.wait_ge(cx.gw.h, cx.gw.n)
        nc.vector.wait_ge(cx.gw.h, cx.gw.n)
        evac_ve = {}
        for rc in range(T):
            for nch in range(4):
                j = rc * 4 + nch
                if j >= 4:
                    nc.tensor.wait_ge(cx.ve.h, evac_ve[j - 4])
                for t in range(T):
                    ins = nc.tensor.matmul(
                        psV[:, nch, :], ln[:, t, rc * 128:(rc + 1) * 128],
                        wv[:, t, nch * 512:(nch + 1) * 512],
                        start=(t == 0), stop=(t == T - 1))
                ins.then_inc(*cx.pe.tick())
                nc.vector.wait_ge(cx.pe.h, cx.pe.n)
                if j >= 2:
                    nc.vector.wait_ge(cx.sd.h, cx.sd.n - 16)
                nc.vector.tensor_add(stg[:, j % 2, :], psV[:, nch, :],
                                     bv[:, nch * 512:(nch + 1) * 512]
                                     ).then_inc(*cx.ve.tick())
                evac_ve[j] = cx.ve.n
                nc.sync.wait_ge(cx.ve.h, cx.ve.n)
                nc.sync.dma_start(
                    out=av_in[rc * 128:(rc + 1) * 128, nch * 512:(nch + 1) * 512],
                    in_=stg[:, j % 2, :]).then_inc(*cx.sd.tick(16))
        nc.sync.wait_ge(cx.sd.h, cx.sd.n)
        cx.barrier()


def emit_attn_inner_a(cx, consts, aq_my, ak_my, av_my, az_in):
    nc = cx.nc
    sc = 1.0 / math.sqrt(128.0)
    with ExitStack() as es:
        kb_ = es.enter_context(nc.sbuf_tensor(un("aa_k"), [128, 2, 32, 128], BF16))
        qb_ = es.enter_context(nc.sbuf_tensor(un("aa_q"), [128, 2, 32, 128], BF16))
        vb_ = es.enter_context(nc.sbuf_tensor(un("aa_v"), [128, 2, 32, 128], BF16))
        pt = es.enter_context(nc.sbuf_tensor(un("aa_p"), [128, 2, 512], BF16))
        sC = es.enter_context(nc.sbuf_tensor(un("aa_c"), [128, 1], F32))
        sml = es.enter_context(nc.sbuf_tensor(un("aa_sml"), [128, 4], F32))
        qss = es.enter_context(nc.sbuf_tensor(un("aa_qss"), [128, 2, 128], BF16))
        rcp = es.enter_context(nc.sbuf_tensor(un("aa_rcp"), [128, 512], F32))
        zt = es.enter_context(nc.sbuf_tensor(un("aa_z"), [128, 2, 512], BF16))
        psS = es.enter_context(nc.psum_tensor(un("aa_psS"), [128, 2, 512], F32))
        psO = es.enter_context(nc.psum_tensor(un("aa_psO"), [128, 512], F32))
        psB = es.enter_context(nc.psum_tensor(un("aa_psB"), [128, 512], F32))
        psR = es.enter_context(nc.psum_tensor(un("aa_psR"), [128, 128], F32))
        psN = es.enter_context(nc.psum_tensor(un("aa_psN"), [1, 512], F32))
        ones1 = consts["ones1"]
        onesr = consts["onesr_f"]
        ident = consts["ident"]
        diag = consts["diag"]
        pe_end = {}
        ld_end = {}
        for inst in range(8):
            b, hp = divmod(inst, 2)
            buf = inst % 2
            if inst >= 2:
                nc.sync.wait_ge(cx.pe.h, pe_end[inst - 2])
            for j in range(NC):
                nc.sync.dma_start(
                    out=qb_[:, buf, j * 4:(j + 1) * 4, :].rearrange(
                        "p c e -> p (c e)"),
                    in_=aq_my[j, hp * 128:(hp + 1) * 128,
                              b * 512:(b + 1) * 512]).then_inc(*cx.sd.tick(16))
                nc.sync.dma_start(
                    out=kb_[:, buf, j * 4:(j + 1) * 4, :].rearrange(
                        "p c e -> p (c e)"),
                    in_=ak_my[j, hp * 128:(hp + 1) * 128,
                              b * 512:(b + 1) * 512]).then_inc(*cx.sd.tick(16))
                for cc_ in range(4):
                    r0 = b * 512 + cc_ * 128
                    nc.sync.dma_start(
                        out=vb_[:, buf, j * 4 + cc_, :],
                        in_=av_my[j, r0:r0 + 128, hp * 128:(hp + 1) * 128]
                    ).then_inc(*cx.sd.tick(16))
            ld_end[inst] = cx.sd.n
            nc.vector.wait_ge(cx.sd.h, ld_end[inst])
            if inst >= 1:
                nc.vector.wait_ge(cx.pe.h, cx.pe.n)
            nc.vector.tensor_copy(qss[:, 0, :], qb_[:, buf, :, 0:4])
            nc.vector.tensor_copy(qss[:, 1, :], kb_[:, buf, :, 0:4]
                                  ).then_inc(*cx.ve.tick())
            nc.tensor.wait_ge(cx.sd.h, ld_end[inst])
            nc.tensor.wait_ge(cx.ve.h, cx.ve.n)
            nc.tensor.matmul(psR[:], qss[:, 0, :], qss[:, 1, :],
                             start=True, stop=True).then_inc(*cx.pe.tick())
            nc.vector.wait_ge(cx.pe.h, cx.pe.n)
            nc.vector.tensor_reduce(sml[:, 0:1], psR[:], AX.X, ALU.max
                                    ).then_inc(*cx.ve.tick())
            nc.tensor.wait_ge(cx.ve.h, cx.ve.n)
            nc.tensor.transpose(psR[0:1, :], sml[:, 0:1], consts["ident_f"][:]
                                ).then_inc(*cx.pe.tick())
            nc.vector.wait_ge(cx.pe.h, cx.pe.n)
            nc.vector.tensor_reduce(sml[0:1, 1:2], psR[0:1, :], AX.X, ALU.max)
            nc.vector.tensor_scalar(sml[0:1, 2:3], sml[0:1, 1:2], -sc, -2.0 * sc,
                                    ALU.mult, ALU.add).then_inc(*cx.ve.tick())
            nc.tensor.wait_ge(cx.ve.h, cx.ve.n)
            nc.tensor.matmul(psR[:, 0:1], onesr[:], sml[0:1, 2:3],
                             start=True, stop=True).then_inc(*cx.pe.tick())
            nc.vector.wait_ge(cx.pe.h, cx.pe.n)
            nc.vector.wait_ge(cx.ac.h, cx.ac.n)
            nc.vector.tensor_copy(sC[:], psR[:, 0:1]).then_inc(*cx.ve.tick())
            nc.scalar.wait_ge(cx.ve.h, cx.ve.n)
            for j in range(8):
                nk = 4 * j + 4
                qrhs = qb_[:, buf, j * 4:(j + 1) * 4, :].rearrange(
                    "p c e -> p (c e)")
                st_tick = {}
                exp_tick = {}
                avs_tick = {}
                ac_base = cx.ac.n

                def emit_st(k):
                    sb = k % 2
                    if k >= 2:
                        nc.tensor.wait_ge(cx.ac.h, exp_tick[k - 2])
                    else:
                        nc.tensor.wait_ge(cx.ac.h, ac_base)
                    nc.tensor.matmul(
                        psS[:, sb, :], kb_[:, buf, k, :], qrhs,
                        start=True, stop=True, skip_group_check=True
                    ).then_inc(*cx.pe.tick())
                    st_tick[k] = cx.pe.n

                emit_st(0)
                for k in range(nk):
                    sb = k % 2
                    if k + 1 < nk:
                        emit_st(k + 1)
                    v_of_k = k - 4 * j
                    if 0 <= v_of_k < 4:
                        nc.vector.wait_ge(cx.pe.h, st_tick[k])
                        nc.vector.tensor_tensor(
                            psS[:, sb, :], psS[:, sb, :], diag[:, v_of_k, :],
                            ALU.add).then_inc(*cx.ve.tick())
                        nc.scalar.wait_ge(cx.ve.h, cx.ve.n)
                    else:
                        nc.scalar.wait_ge(cx.pe.h, st_tick[k])
                    if k >= 2:
                        nc.scalar.wait_ge(cx.pe.h, avs_tick[k - 2])
                    nc.scalar.activation(pt[:, sb, :], psS[:, sb, :], AF.Exp,
                                         bias=sC[:, 0:1], scale=sc
                                         ).then_inc(*cx.ac.tick())
                    exp_tick[k] = cx.ac.n
                    nc.tensor.wait_ge(cx.ac.h, exp_tick[k])
                    nc.tensor.matmul(psO[:], vb_[:, buf, k, :], pt[:, sb, :],
                                     start=(k == 0), stop=(k == nk - 1),
                                     skip_group_check=True)
                    nc.tensor.matmul(psN[:], ones1[:], pt[:, sb, :],
                                     start=(k == 0), stop=(k == nk - 1),
                                     skip_group_check=True).then_inc(*cx.pe.tick())
                    avs_tick[k] = cx.pe.n
                nc.vector.wait_ge(cx.pe.h, cx.pe.n)
                nc.vector.reciprocal(rcp[0:1, :], psN[:]).then_inc(*cx.ve.tick())
                nc.tensor.wait_ge(cx.ve.h, cx.ve.n)
                nc.tensor.matmul(psB[:], onesr[:], rcp[0:1, :],
                                 start=True, stop=True).then_inc(*cx.pe.tick())
                nc.vector.wait_ge(cx.pe.h, cx.pe.n)
                if j >= 2:
                    nc.vector.wait_ge(cx.sd.h, cx.sd.n)
                nc.vector.tensor_copy(rcp[:], psB[:])
                nc.vector.tensor_mul(zt[:, j % 2, :], psO[:], rcp[:]
                                     ).then_inc(*cx.ve.tick())
                nc.sync.wait_ge(cx.ve.h, cx.ve.n)
                nc.sync.dma_start(
                    out=az_in[hp * 128:(hp + 1) * 128,
                              b * 4096 + j * 512:b * 4096 + (j + 1) * 512],
                    in_=zt[:, j % 2, :]).then_inc(*cx.sd.tick(16))
                nc.tensor.wait_ge(cx.ve.h, cx.ve.n)
            pe_end[inst] = cx.pe.n
        nc.sync.wait_ge(cx.sd.h, cx.sd.n)
        cx.barrier()


def emit_outproj(cx, w_ap, bias_ap, z_all, xres, out_fp32=None):
    nc = cx.nc
    with ExitStack() as es:
        zs = es.enter_context(nc.sbuf_tensor(un("op_z"), [128, H, B, 512], BF16))
        wo = es.enter_context(nc.sbuf_tensor(un("op_w"), [128, 2, H, 128], BF16))
        bo = es.enter_context(nc.sbuf_tensor(un("op_b"), [128, T], F32))
        xr = es.enter_context(nc.sbuf_tensor(un("op_xr"), [128, 2, 512], BF16))
        stg = es.enter_context(nc.sbuf_tensor(un("op_stg"), [128, 2, 512], BF16))
        stgf = es.enter_context(nc.sbuf_tensor(un("op_stgf"), [128, 2, 512], F32))
        psO = es.enter_context(nc.psum_tensor(un("op_ps"), [128, 2, 512], F32))
        if bias_ap is not None:
            nc.gpsimd.dma_start(out=bo[:], in_=bias_ap).then_inc(*cx.gw.tick(16))
        for h in range(H):
            nc.scalar.dma_start(
                out=zs[:, h, :, :],
                in_=z_all[h * 128:(h + 1) * 128, :, :]).then_inc(*cx.ad.tick(16))
        nc.tensor.wait_ge(cx.ad.h, cx.ad.n)
        ev_done = {}
        for dt in range(T):
            dbuf = dt % 2
            if dt >= 2:
                nc.gpsimd.wait_ge(cx.pe.h, ev_done.get(("mm", dt - 2), 0))
            nc.gpsimd.dma_start(
                out=wo[:, dbuf, :, :],
                in_=w_ap[:, dt, :, :].rearrange("h e q -> e h q")
            ).then_inc(*cx.gw.tick(16))
            nc.tensor.wait_ge(cx.gw.h, cx.gw.n)
            for b_ in range(B):
                jj = dt * B + b_
                if jj >= 2:
                    nc.tensor.wait_ge(cx.ve.h, ev_done.get(("ev", jj - 2), 0))
                for h in range(H):
                    ins = nc.tensor.matmul(
                        psO[:, jj % 2, :], wo[:, dbuf, h, :], zs[:, h, b_, :],
                        start=(h == 0), stop=(h == H - 1))
                ins.then_inc(*cx.pe.tick())
                ev_done[("mm", dt)] = cx.pe.n
                nc.sync.dma_start(
                    out=xr[:, jj % 2, :],
                    in_=xres[dt, :, b_ * 512:(b_ + 1) * 512]
                ).then_inc(*cx.sd.tick(16))
                nc.vector.wait_ge(cx.pe.h, cx.pe.n)
                nc.vector.wait_ge(cx.sd.h, cx.sd.n)
                if jj >= 2:
                    pass
                if bias_ap is not None:
                    nc.vector.scalar_tensor_tensor(
                        stg[:, jj % 2, :], psO[:, jj % 2, :], bo[:, dt:dt + 1],
                        xr[:, jj % 2, :], ALU.add, ALU.add)
                else:
                    nc.vector.tensor_add(stg[:, jj % 2, :], psO[:, jj % 2, :],
                                         xr[:, jj % 2, :])
                if out_fp32 is not None:
                    if bias_ap is not None:
                        nc.vector.scalar_tensor_tensor(
                            stgf[:, jj % 2, :], psO[:, jj % 2, :],
                            bo[:, dt:dt + 1], xr[:, jj % 2, :], ALU.add, ALU.add)
                    else:
                        nc.vector.tensor_add(stgf[:, jj % 2, :],
                                             psO[:, jj % 2, :], xr[:, jj % 2, :])
                nc.vector.engine_nop().then_inc(*cx.ve.tick())
                ev_done[("ev", jj)] = cx.ve.n
                nc.sync.wait_ge(cx.ve.h, cx.ve.n)
                nc.sync.dma_start(
                    out=xres[dt, :, b_ * 512:(b_ + 1) * 512],
                    in_=stg[:, jj % 2, :]).then_inc(*cx.sd.tick(16))
                if out_fp32 is not None:
                    nc.sync.dma_start(
                        out=out_fp32[dt, :, b_ * 512:(b_ + 1) * 512],
                        in_=stgf[:, jj % 2, :]).then_inc(*cx.sd.tick(16))
        nc.sync.wait_ge(cx.sd.h, cx.sd.n)
        cx.barrier()


def emit_ffn(cx, consts, inp, xres, vffn, out_fp32=None, stop_after=None,
             dbg=None):
    def dump(name, sb_ap):
        if dbg and name in dbg:
            nc = cx.nc
            nc.gpsimd.dma_start(out=dbg[name], in_=sb_ap).then_inc(*cx.gw.tick(16))
            nc.gpsimd.wait_ge(cx.gw.h, cx.gw.n)
            cx.barrier()
    """Whole FFN sublayer: LN -> conv -> v/router -> q,k/rope/dots -> o/LN ->
    gate -> Wout + residual. All row-local."""
    nc = cx.nc
    onesr = consts["onesr_f"]
    identb = consts["ident"]
    with ExitStack() as big:
        xmix = big.enter_context(nc.sbuf_tensor(un("f_xmix"), [128, T, RW], BF16))
        with ExitStack() as es:
            ln = es.enter_context(nc.sbuf_tensor(un("f_ln"), [128, T, 512 + RW], BF16))
            emit_ln(cx, {"ln": ln}, xres, consts, pad_rows=512)
            cx.barrier()
            cw = es.enter_context(nc.sbuf_tensor(un("f_cw"), [128, T, 2], F32))
            pre = es.enter_context(nc.sbuf_tensor(un("f_pre"), [128, 2, 512], F32))
            nc.gpsimd.dma_start(out=cw[:], in_=inp["f_conv"].rearrange(
                "t q e -> q t e")).then_inc(*cx.gw.tick(16))
            nc.vector.wait_ge(cx.gw.h, cx.gw.n)
            for t in range(T):
                for ch in range(4):
                    r0 = ch * 512
                    i = t * 4 + ch
                    if i >= 2:
                        nc.vector.wait_ge(cx.ac.h, cx.ac.n - 1)
                    nc.vector.tensor_scalar_mul(
                        pre[:, i % 2, :], ln[:, t, 512 + r0:1024 + r0],
                        cw[:, t, 1:2])
                    nc.vector.scalar_tensor_tensor(
                        pre[:, i % 2, :], ln[:, t, r0:r0 + 512], cw[:, t, 0:1],
                        pre[:, i % 2, :], ALU.mult, ALU.add
                    ).then_inc(*cx.ve.tick())
                    nc.scalar.wait_ge(cx.ve.h, cx.ve.n)
                    nc.scalar.activation(xmix[:, t, r0:r0 + 512], pre[:, i % 2, :],
                                         AF.Silu).then_inc(*cx.ac.tick())
            nc.scalar.wait_ge(cx.ac.h, cx.ac.n)
            cx.barrier()

        if stop_after == "conv":
            return
        rwrep = big.enter_context(nc.sbuf_tensor(un("f_rwrep"), [128, D], F32))
        ftd = big.enter_context(nc.sbuf_tensor(un("f_ftd"), [128, 16, H], F32))
        with ExitStack() as es:
            rw2 = es.enter_context(nc.sbuf_tensor(un("f_rw2"), [1, D, 2], F32))
            rwrow = es.enter_context(nc.sbuf_tensor(un("f_rwrow"), [1, D], F32))
            psB = es.enter_context(nc.psum_tensor(un("f_psb"), [128, 512], F32))
            nc.gpsimd.dma_start(out=rw2[:], in_=inp["f_rw"].rearrange(
                "h j e -> (h j) e")[None, :, :]).then_inc(*cx.gw.tick(16))
            nc.vector.wait_ge(cx.gw.h, cx.gw.n)
            nc.vector.tensor_sub(rwrow[:], rw2[:, :, 0], rw2[:, :, 1]
                                 ).then_inc(*cx.ve.tick())
            for nch in range(4):
                nc.tensor.wait_ge(cx.ve.h, cx.ve.n)
                nc.tensor.matmul(psB[:], onesr[:],
                                 rwrow[:, nch * 512:(nch + 1) * 512],
                                 start=True, stop=True).then_inc(*cx.pe.tick())
                nc.vector.wait_ge(cx.pe.h, cx.pe.n)
                nc.vector.tensor_copy(rwrep[:, nch * 512:(nch + 1) * 512],
                                      psB[:]).then_inc(*cx.ve.tick())
            nc.tensor.wait_ge(cx.ve.h, cx.ve.n)
            cx.barrier()

        if stop_after == "rw":
            return
        with ExitStack() as es:
            wv = es.enter_context(nc.sbuf_tensor(un("f_wv"), [128, T, D], BF16))
            stg = es.enter_context(nc.sbuf_tensor(un("f_stgv"), [128, 2, 512], BF16))
            dtm = es.enter_context(nc.sbuf_tensor(un("f_dtm"), [128, 512], F32))
            psV = es.enter_context(nc.psum_tensor(un("f_psv"), [128, 4, 512], F32))
            for t in range(T):
                nc.gpsimd.dma_start(out=wv[:, t, :], in_=inp["f_wv"][t]
                                    ).then_inc(*cx.gw.tick(16))
            nc.tensor.wait_ge(cx.gw.h, cx.gw.n)
            evac_ve = {}
            for rc in range(T):
                for nch in range(4):
                    j = rc * 4 + nch
                    if j >= 4:
                        nc.tensor.wait_ge(cx.ve.h, evac_ve[j - 4])
                    for t in range(T):
                        ins = nc.tensor.matmul(
                            psV[:, nch, :], xmix[:, t, rc * 128:(rc + 1) * 128],
                            wv[:, t, nch * 512:(nch + 1) * 512],
                            start=(t == 0), stop=(t == T - 1))
                    ins.then_inc(*cx.pe.tick())
                    nc.vector.wait_ge(cx.pe.h, cx.pe.n)
                    if j >= 2:
                        nc.vector.wait_ge(cx.sd.h, cx.sd.n - 16)
                    nc.vector.tensor_copy(stg[:, j % 2, :], psV[:, nch, :])
                    nc.vector.tensor_mul(dtm[:], psV[:, nch, :],
                                         rwrep[:, nch * 512:(nch + 1) * 512])
                    nc.vector.tensor_reduce(
                        ftd[:, rc, nch * 4:(nch + 1) * 4],
                        dtm[:].rearrange("p (h j) -> p h j", j=128),
                        AX.X, ALU.add).then_inc(*cx.ve.tick())
                    evac_ve[j] = cx.ve.n
                    nc.sync.wait_ge(cx.ve.h, cx.ve.n)
                    nc.sync.dma_start(
                        out=vffn[rc * 128:(rc + 1) * 128,
                                 nch * 512:(nch + 1) * 512],
                        in_=stg[:, j % 2, :]).then_inc(*cx.sd.tick(16))
            nc.sync.wait_ge(cx.sd.h, cx.sd.n)
            cx.barrier()

        if stop_after == "v":
            return
        CF = big.enter_context(nc.sbuf_tensor(un("f_cf"), [128, 4, 4, 4, H], F32))
        with ExitStack() as es:
            fq = es.enter_context(nc.sbuf_tensor(un("f_q"), [128, T, FDK], BF16))
            fk = es.enter_context(nc.sbuf_tensor(un("f_k"), [128, T, FDK], BF16))
            with ExitStack() as esw:
                wqk = esw.enter_context(nc.sbuf_tensor(un("f_wqk"), [128, T, FDK], BF16))
                psQ = esw.enter_context(nc.psum_tensor(un("f_psq"), [128, 4, 512], F32))
                for kind in range(2):
                    wsrc_ap = inp["f_wq"] if kind == 0 else inp["f_wk"]
                    dst = fq if kind == 0 else fk
                    if kind == 1:
                        nc.gpsimd.wait_ge(cx.ac.h, cx.ac.n)
                    for t in range(T):
                        nc.gpsimd.dma_start(out=wqk[:, t, :], in_=wsrc_ap[t]
                                            ).then_inc(*cx.gw.tick(16))
                    nc.tensor.wait_ge(cx.gw.h, cx.gw.n)
                    done = {}
                    for rc in range(T):
                        for half in range(2):
                            j = rc * 2 + half
                            bank = j % 4
                            if j >= 4:
                                nc.tensor.wait_ge(cx.ac.h, done[j - 4])
                            for t in range(T):
                                ins = nc.tensor.matmul(
                                    psQ[:, bank, :],
                                    xmix[:, t, rc * 128:(rc + 1) * 128],
                                    wqk[:, t, half * 512:(half + 1) * 512],
                                    start=(t == 0), stop=(t == T - 1))
                            ins.then_inc(*cx.pe.tick())
                            nc.scalar.wait_ge(cx.pe.h, cx.pe.n)
                            nc.scalar.activation(
                                dst[:, rc, half * 512:(half + 1) * 512],
                                psQ[:, bank, :],
                                AF.Silu if kind == 0 else AF.Identity
                            ).then_inc(*cx.ac.tick())
                            done[j] = cx.ac.n
            nc.scalar.wait_ge(cx.ac.h, cx.ac.n)
            cx.barrier()
            cosr = es.enter_context(nc.sbuf_tensor(un("f_cos"), [128, B, H * 32], F32))
            sinr = es.enter_context(nc.sbuf_tensor(un("f_sin"), [128, B, H * 32], F32))
            s1 = es.enter_context(nc.sbuf_tensor(un("f_s1"), [128, 2, H * 32], F32))
            nc.gpsimd.dma_start(out=cosr[:], in_=inp["ffn_cos"].rearrange(
                "b q c -> q b c")).then_inc(*cx.gw.tick(16))
            nc.gpsimd.dma_start(out=sinr[:], in_=inp["ffn_sin"].rearrange(
                "b q c -> q b c")).then_inc(*cx.gw.tick(16))
            nc.vector.wait_ge(cx.gw.h, cx.gw.n)
            v = VS(cx)
            for rc in range(T):
                b_ = rc // 4
                for buf_ in (fq, fk):
                    xh = buf_[:, rc, :].rearrange("p (h d) -> p h d", d=64)
                    x1 = xh[:, :, 0:32]
                    x2 = xh[:, :, 32:64]
                    cosv = cosr[:, b_, :].rearrange("p (h d) -> p h d", d=32)
                    sinv = sinr[:, b_, :].rearrange("p (h d) -> p h d", d=32)
                    s1v = s1[:, 0, :].rearrange("p (h d) -> p h d", d=32)
                    s2v = s1[:, 1, :].rearrange("p (h d) -> p h d", d=32)
                    v.tensor_mul(s1v, x1, cosv)
                    v.tensor_mul(s2v, x2, sinv)
                    v.tensor_sub(s1v, s1v, s2v)
                    v.tensor_mul(s2v, x1, sinv)
                    v.tensor_mul(x2, x2, cosv)
                    v.tensor_add(x2, x2, s2v)
                    v.tensor_copy(x1, s1v)
            cx.barrier()
            dump("fq", fq[:])
            dump("fk", fk[:])

            i0 = es.enter_context(nc.sbuf_tensor(un("f_i0"), [128, 16, H], F32))
            al = es.enter_context(nc.sbuf_tensor(un("f_al"), [128, 16, H], F32))
            cum = es.enter_context(nc.sbuf_tensor(un("f_cum"), [128, 4, 2, H], F32))
            tmp = es.enter_context(nc.sbuf_tensor(un("f_tmp"), [128, 4, H], F32))
            dtm = es.enter_context(nc.sbuf_tensor(un("f_dt2"), [128, FDK], F32))
            v = VS(cx)
            s = nc.scalar
            v.tensor_single_scalar(i0[:], ftd[:], 0.0, ALU.is_ge)
            nc.gpsimd.wait_ge(cx.ve.h, cx.ve.n)
            dump("fi0b", i0[:])
            s.wait_ge(cx.ve.h, cx.ve.n)
            s.activation(al[:], ftd[:], AF.Abs).then_inc(*cx.ac.tick())
            s.wait_ge(cx.ac.h, cx.ac.n)
            s.activation(al[:], al[:], AF.Sigmoid).then_inc(*cx.ac.tick())
            v.wait_ge(cx.ac.h, cx.ac.n)
            for g in range(4):
                for b_ in range(B):
                    rt = b_ * 4 + g
                    if b_ == 0:
                        v.tensor_copy(cum[:, g, 0, :], i0[:, rt, :])
                        v.tensor_scalar(cum[:, g, 1, :], i0[:, rt, :], -1.0, 1.0,
                                        ALU.mult, ALU.add)
                    else:
                        v.tensor_add(cum[:, g, 0, :], cum[:, g, 0, :],
                                     i0[:, rt, :])
                        v.tensor_scalar(tmp[:, 0, :], i0[:, rt, :], -1.0, 1.0,
                                        ALU.mult, ALU.add)
                        v.tensor_add(cum[:, g, 1, :], cum[:, g, 1, :],
                                     tmp[:, 0, :])
                    v.tensor_mul(tmp[:, 1, :], i0[:, rt, :], cum[:, g, 0, :])
                    v.tensor_scalar(tmp[:, 2, :], i0[:, rt, :], -1.0, 1.0,
                                    ALU.mult, ALU.add)
                    v.tensor_mul(tmp[:, 2, :], tmp[:, 2, :], cum[:, g, 1, :])
                    v.tensor_add(tmp[:, 1, :], tmp[:, 1, :], tmp[:, 2, :])
                    v.tensor_scalar_max(tmp[:, 1, :], tmp[:, 1, :], 1.0)
                    v.reciprocal(tmp[:, 1, :], tmp[:, 1, :])
                    v.tensor_mul(tmp[:, 1, :], tmp[:, 1, :], al[:, rt, :])
                    v.tensor_scalar_mul(al[:, rt, :], tmp[:, 1, :],
                                        1.0 / math.sqrt(FHDK))
            nc.gpsimd.wait_ge(cx.ve.h, cx.ve.n)
            dump("fi0c", i0[:])
            dump("falc", al[:])
            for g in range(4):
                for l in range(B):
                    for m in range(l + 1):
                        rl = l * 4 + g
                        rm = m * 4 + g
                        v.tensor_mul(dtm[:], fq[:, rl, :], fk[:, rm, :])
                        v.tensor_reduce(tmp[:, 3, :],
                                        dtm[:].rearrange("p (h d) -> p h d", d=64),
                                        AX.X, ALU.add)
                        v.tensor_mul(tmp[:, 0, :], i0[:, rl, :], i0[:, rm, :])
                        v.tensor_add(tmp[:, 1, :], i0[:, rl, :], i0[:, rm, :])
                        v.tensor_scalar(tmp[:, 1, :], tmp[:, 1, :], -1.0, 1.0,
                                        ALU.mult, ALU.add)
                        v.tensor_scalar(tmp[:, 0, :], tmp[:, 0, :], 2.0, None,
                                        ALU.mult)
                        v.tensor_add(tmp[:, 0, :], tmp[:, 0, :], tmp[:, 1, :])
                        v.tensor_mul(tmp[:, 3, :], tmp[:, 3, :], tmp[:, 0, :])
                        v.tensor_mul(CF[:, g, l, m, :], tmp[:, 3, :],
                                     al[:, rl, :])
            cx.barrier()

        if stop_after == "qk":
            return
        olnT = big.enter_context(nc.sbuf_tensor(un("f_olnT"), [128, T, RW], BF16))
        with ExitStack() as es:
            vt = es.enter_context(nc.sbuf_tensor(un("f_vt"), [128, B, D], BF16))
            ob = es.enter_context(nc.sbuf_tensor(un("f_o"), [128, D], F32))
            osq = es.enter_context(nc.sbuf_tensor(un("f_osq"), [128, D], F32))
            stats = es.enter_context(nc.sbuf_tensor(un("f_ost"), [128, 6, H], F32))
            opos = es.enter_context(nc.sbuf_tensor(un("f_opos"), [128, D], BF16))
            psT = es.enter_context(nc.psum_tensor(un("f_pst"), [128, 2, 1024], BF16))
            v = VS(cx)
            ve_copy = {}
            for g in range(4):
                nc.sync.wait_ge(cx.ve.h, cx.ve.n)
                for m in range(B):
                    nc.sync.dma_start(out=vt[:, m, :],
                                      in_=vffn[(m * 4 + g) * 128:
                                               (m * 4 + g + 1) * 128, :]
                                      ).then_inc(*cx.sd.tick(16))
                v.wait_ge(cx.sd.h, cx.sd.n)
                for l in range(B):
                    for m in range(l + 1):
                        cfb = CF[:, g, l, m, :].unsqueeze(2).broadcast_to(
                            [128, H, 128])
                        vth = vt[:, m, :].rearrange("p (h d) -> p h d", d=128)
                        if m == 0:
                            v.tensor_mul(ob[:].rearrange("p (h d) -> p h d",
                                                         d=128), vth, cfb)
                        else:
                            v.tensor_mul(osq[:].rearrange("p (h d) -> p h d",
                                                          d=128), vth, cfb)
                            v.tensor_add(ob[:], ob[:], osq[:])
                    obh = ob[:].rearrange("p (h d) -> p h d", d=128)
                    v.tensor_reduce(stats[:, 0, :], obh, AX.X, ALU.add)
                    v.tensor_scalar_mul(stats[:, 0, :], stats[:, 0, :], 1.0 / 128)
                    nc.scalar.wait_ge(cx.ve.h, cx.ve.n)
                    nc.scalar.activation(osq[:], ob[:], AF.Square
                                         ).then_inc(*cx.ac.tick())
                    v.wait_ge(cx.ac.h, cx.ac.n)
                    v.tensor_reduce(stats[:, 1, :],
                                    osq[:].rearrange("p (h d) -> p h d", d=128),
                                    AX.X, ALU.add)
                    v.tensor_scalar_mul(stats[:, 1, :], stats[:, 1, :], 1.0 / 128)
                    v.tensor_mul(stats[:, 2, :], stats[:, 0, :], stats[:, 0, :])
                    v.tensor_sub(stats[:, 2, :], stats[:, 1, :], stats[:, 2, :])
                    v.tensor_scalar_add(stats[:, 2, :], stats[:, 2, :], EPS)
                    nc.scalar.wait_ge(cx.ve.h, cx.ve.n)
                    nc.scalar.activation(stats[:, 4, :], stats[:, 2, :], AF.Sqrt
                                         ).then_inc(*cx.ac.tick())
                    v.wait_ge(cx.ac.h, cx.ac.n)
                    v.reciprocal(stats[:, 3, :], stats[:, 4, :])
                    v.tensor_mul(stats[:, 4, :], stats[:, 3, :], stats[:, 3, :])
                    v.tensor_mul(stats[:, 4, :], stats[:, 4, :], stats[:, 2, :])
                    v.tensor_scalar(stats[:, 4, :], stats[:, 4, :], -0.5, 1.5,
                                    ALU.mult, ALU.add)
                    v.tensor_mul(stats[:, 3, :], stats[:, 3, :], stats[:, 4, :])
                    mb = stats[:, 0, :].unsqueeze(2).broadcast_to([128, H, 128])
                    rb = stats[:, 3, :].unsqueeze(2).broadcast_to([128, H, 128])
                    v.tensor_sub(osq[:].rearrange("p (h d) -> p h d", d=128),
                                 obh, mb)
                    v.tensor_mul(opos[:].rearrange("p (h d) -> p h d", d=128),
                                 osq[:].rearrange("p (h d) -> p h d", d=128),
                                 rb)
                    rt = l * 4 + g
                    nc.tensor.wait_ge(cx.ve.h, cx.ve.n)
                    for h in range(H):
                        if h >= 2:
                            nc.tensor.wait_ge(cx.ve.h, ve_copy[h - 2])
                        nc.tensor.transpose(psT[:, h % 2, 0:128],
                                            opos[:, h * 128:(h + 1) * 128],
                                            identb[:]).then_inc(*cx.pe.tick())
                        v.wait_ge(cx.pe.h, cx.pe.n)
                        v.tensor_copy(olnT[:, h, rt * 128:(rt + 1) * 128],
                                      psT[:, h % 2, 0:128])
                        ve_copy[h] = cx.ve.n
            cx.barrier()

        if stop_after == "o":
            return
        dump("folnT", olnT[:])
        with ExitStack() as es:
            wg = es.enter_context(nc.sbuf_tensor(un("f_wg"), [128, 2, T, 128], BF16))
            sg = es.enter_context(nc.sbuf_tensor(un("f_sg"), [128, 2, 512], BF16))
            psG = es.enter_context(nc.psum_tensor(un("f_psg"), [128, 2, 512], F32))
            mm_done = {}
            ev_done = {}
            for gt in range(T):
                dbuf = gt % 2
                if gt >= 2:
                    nc.gpsimd.wait_ge(cx.pe.h, mm_done[gt - 2])
                nc.gpsimd.dma_start(
                    out=wg[:, dbuf, :, :],
                    in_=inp["f_wg"][:, :, gt, :].rearrange("t q e -> q t e")
                ).then_inc(*cx.gw.tick(16))
                nc.tensor.wait_ge(cx.gw.h, cx.gw.n)
                for nch in range(4):
                    jj = gt * 4 + nch
                    if jj >= 2:
                        nc.tensor.wait_ge(cx.ve.h, ev_done.get(jj - 2, 0))
                    for t in range(T):
                        ins = nc.tensor.matmul(
                            psG[:, jj % 2, :], wg[:, dbuf, t, :],
                            xmix[:, t, nch * 512:(nch + 1) * 512],
                            start=(t == 0), stop=(t == T - 1))
                    ins.then_inc(*cx.pe.tick())
                    mm_done[gt] = cx.pe.n
                    nc.scalar.wait_ge(cx.pe.h, cx.pe.n)
                    nc.scalar.activation(sg[:, jj % 2, :], psG[:, jj % 2, :],
                                         AF.Silu).then_inc(*cx.ac.tick())
                    nc.vector.wait_ge(cx.ac.h, cx.ac.n)
                    nc.vector.tensor_mul(
                        olnT[:, gt, nch * 512:(nch + 1) * 512],
                        olnT[:, gt, nch * 512:(nch + 1) * 512],
                        sg[:, jj % 2, :]).then_inc(*cx.ve.tick())
                    ev_done[jj] = cx.ve.n
            nc.vector.wait_ge(cx.ve.h, cx.ve.n)
            cx.barrier()

        dump("folnG", olnT[:])
        if stop_after == "g":
            return
        with ExitStack() as es:
            wo = es.enter_context(nc.sbuf_tensor(un("f_wo"), [128, 2, T, 128], BF16))
            xr = es.enter_context(nc.sbuf_tensor(un("f_xr"), [128, 2, 512], BF16))
            stg = es.enter_context(nc.sbuf_tensor(un("f_stgo"), [128, 2, 512], BF16))
            stgf = es.enter_context(nc.sbuf_tensor(un("f_stgf"), [128, 2, 512], F32))
            psO = es.enter_context(nc.psum_tensor(un("f_pso"), [128, 2, 512], F32))
            mm_done = {}
            ev_done = {}
            for dt in range(T):
                dbuf = dt % 2
                if dt >= 2:
                    nc.gpsimd.wait_ge(cx.pe.h, mm_done[dt - 2])
                nc.gpsimd.dma_start(
                    out=wo[:, dbuf, :, :],
                    in_=inp["f_wout"][:, :, dt, :].rearrange("t q e -> q t e")
                ).then_inc(*cx.gw.tick(16))
                nc.tensor.wait_ge(cx.gw.h, cx.gw.n)
                for nch in range(4):
                    jj = dt * 4 + nch
                    if jj >= 2:
                        nc.tensor.wait_ge(cx.ve.h, ev_done.get(jj - 2, 0))
                    for t in range(T):
                        ins = nc.tensor.matmul(
                            psO[:, jj % 2, :], wo[:, dbuf, t, :],
                            olnT[:, t, nch * 512:(nch + 1) * 512],
                            start=(t == 0), stop=(t == T - 1))
                    ins.then_inc(*cx.pe.tick())
                    mm_done[dt] = cx.pe.n
                    nc.sync.dma_start(
                        out=xr[:, jj % 2, :],
                        in_=xres[dt, :, nch * 512:(nch + 1) * 512]
                    ).then_inc(*cx.sd.tick(16))
                    nc.vector.wait_ge(cx.pe.h, cx.pe.n)
                    nc.vector.wait_ge(cx.sd.h, cx.sd.n)
                    nc.vector.tensor_add(stg[:, jj % 2, :], psO[:, jj % 2, :],
                                         xr[:, jj % 2, :])
                    if out_fp32 is not None:
                        nc.vector.tensor_add(stgf[:, jj % 2, :],
                                             psO[:, jj % 2, :], xr[:, jj % 2, :])
                    nc.vector.engine_nop().then_inc(*cx.ve.tick())
                    ev_done[jj] = cx.ve.n
                    nc.sync.wait_ge(cx.ve.h, cx.ve.n)
                    nc.sync.dma_start(
                        out=xres[dt, :, nch * 512:(nch + 1) * 512],
                        in_=stg[:, jj % 2, :]).then_inc(*cx.sd.tick(16))
                    if out_fp32 is not None:
                        nc.sync.dma_start(
                            out=out_fp32[dt, :, nch * 512:(nch + 1) * 512],
                            in_=stgf[:, jj % 2, :]).then_inc(*cx.sd.tick(16))
            nc.sync.wait_ge(cx.sd.h, cx.sd.n)
            cx.barrier()


def emit_mla_proj(cx, ln, inp, m_in):
    nc = cx.nc
    with ExitStack() as es:
        wqa = es.enter_context(nc.sbuf_tensor(un("m_wqa"), [128, T, 4, 128], BF16))
        wkva = es.enter_context(nc.sbuf_tensor(un("m_wkva"), [128, T, 576], BF16))
        stg = es.enter_context(nc.sbuf_tensor(un("m_stg"), [128, 2, 512], BF16))
        stgp = es.enter_context(nc.sbuf_tensor(un("m_stgp"), [64, 2, 512], BF16))
        kct = es.enter_context(nc.sbuf_tensor(un("m_kc"), [32, 2, 512], F32))
        psA = es.enter_context(nc.psum_tensor(un("m_ps"), [128, 2, 512], F32))
        psP = es.enter_context(nc.psum_tensor(un("m_psp"), [128, 512], F32))
        cosl = es.enter_context(nc.sbuf_tensor(un("m_cosl"), [32, SL], F32))
        sinl = es.enter_context(nc.sbuf_tensor(un("m_sinl"), [32, SL], F32))
        nc.gpsimd.dma_start(out=wqa[:], in_=inp["wqa"].rearrange(
            "t r q e -> q t r e")).then_inc(*cx.gw.tick(16))
        nc.gpsimd.dma_start(out=wkva[:], in_=inp["wkva"].rearrange(
            "t q e -> q t e")).then_inc(*cx.gw.tick(16))
        nc.gpsimd.dma_start(out=cosl[:], in_=inp["kpe_cos"]).then_inc(*cx.gw.tick(16))
        nc.gpsimd.dma_start(out=sinl[:], in_=inp["kpe_sin"]).then_inc(*cx.gw.tick(16))
        nc.tensor.wait_ge(cx.gw.h, cx.gw.n)
        nc.vector.wait_ge(cx.gw.h, cx.gw.n)
        done = {}
        for o in range(8):
            for nch in range(4):
                j = o * 4 + nch
                if j >= 2:
                    nc.tensor.wait_ge(cx.ac.h, done[j - 2])
                for t in range(T):
                    if o < 4:
                        lhsT = wqa[:, t, o, :]
                    else:
                        lhsT = wkva[:, t, (o - 4) * 128:(o - 3) * 128]
                    ins = nc.tensor.matmul(
                        psA[:, j % 2, :], lhsT,
                        ln[:, t, nch * 512:(nch + 1) * 512],
                        start=(t == 0), stop=(t == T - 1))
                ins.then_inc(*cx.pe.tick())
                nc.scalar.wait_ge(cx.pe.h, cx.pe.n)
                if j >= 2:
                    nc.scalar.wait_ge(cx.sd.h, cx.sd.n - 16)
                nc.scalar.activation(stg[:, j % 2, :], psA[:, j % 2, :],
                                     AF.Identity).then_inc(*cx.ac.tick())
                done[j] = cx.ac.n
                nc.sync.wait_ge(cx.ac.h, cx.ac.n)
                nc.sync.dma_start(
                    out=m_in[o * 128:(o + 1) * 128, nch * 512:(nch + 1) * 512],
                    in_=stg[:, j % 2, :]).then_inc(*cx.sd.tick(16))
        vdone = {}
        for nch in range(4):
            if nch >= 1:
                nc.tensor.wait_ge(cx.ve.h, vdone.get(nch - 1, 0))
            for t in range(T):
                ins = nc.tensor.matmul(
                    psP[0:64, :], wkva[:, t, 512:576],
                    ln[:, t, nch * 512:(nch + 1) * 512],
                    start=(t == 0), stop=(t == T - 1))
            ins.then_inc(*cx.pe.tick())
            v = VS(cx)
            v.wait_ge(cx.pe.h, cx.pe.n)
            if nch >= 2:
                v.wait_ge(cx.sd.h, cx.sd.n - 16)
            v.tensor_mul(kct[:, 0, :], psP[0:32, :], cosl[:])
            v.tensor_mul(kct[:, 1, :], psP[32:64, :], sinl[:])
            v.tensor_sub(kct[:, 0, :], kct[:, 0, :], kct[:, 1, :])
            v.tensor_copy(stgp[0:32, nch % 2, :], kct[:, 0, :])
            v.tensor_mul(kct[:, 0, :], psP[0:32, :], sinl[:])
            v.tensor_mul(kct[:, 1, :], psP[32:64, :], cosl[:])
            v.tensor_add(kct[:, 0, :], kct[:, 0, :], kct[:, 1, :])
            v.tensor_copy(stgp[32:64, nch % 2, :], kct[:, 0, :])
            vdone[nch] = cx.ve.n
            nc.sync.wait_ge(cx.ve.h, cx.ve.n)
            nc.sync.dma_start(
                out=m_in[1024:1088, nch * 512:(nch + 1) * 512],
                in_=stgp[:, nch % 2, :]).then_inc(*cx.sd.tick(16))
        nc.sync.wait_ge(cx.sd.h, cx.sd.n)
        cx.barrier()


def emit_mla_expand(cx, consts, inp, m_all, bq, kn, bv):
    nc = cx.nc
    with ExitStack() as es:
        wqb = es.enter_context(nc.sbuf_tensor(un("x_wqb"), [128, 4, 384], BF16))
        wkk = es.enter_context(nc.sbuf_tensor(un("x_wkk"), [128, 4, 256], BF16))
        wkv = es.enter_context(nc.sbuf_tensor(un("x_wkv"), [128, 4, 256], BF16))
        qa = es.enter_context(nc.sbuf_tensor(un("x_qa"), [128, 2, 4, 512], BF16))
        co = es.enter_context(nc.sbuf_tensor(un("x_co"), [128, 2, 4, 512], BF16))
        stg = es.enter_context(nc.sbuf_tensor(un("x_stg"), [128, 6, 512], BF16))
        rop = es.enter_context(nc.sbuf_tensor(un("x_rop"), [64, 4, 512], F32))
        psN = es.enter_context(nc.psum_tensor(un("x_psn"), [128, 2, 512], F32))
        psE = es.enter_context(nc.psum_tensor(un("x_pse"), [128, 512], F32))
        psK = es.enter_context(nc.psum_tensor(un("x_psk"), [128, 2, 512], F32))
        psV = es.enter_context(nc.psum_tensor(un("x_psv"), [128, 2, 512], F32))
        mcos = es.enter_context(nc.sbuf_tensor(un("x_mcos"), [64, S], F32))
        msin = es.enter_context(nc.sbuf_tensor(un("x_msin"), [64, S], F32))
        nc.gpsimd.dma_start(out=mcos[:], in_=inp["mla_cos2"]
                            ).then_inc(*cx.gw.tick(16))
        nc.gpsimd.dma_start(out=msin[:], in_=inp["mla_sin2"]
                            ).then_inc(*cx.gw.tick(16))
        me = nc.gpsimd.partition_id()
        nc.gpsimd.dma_start(
            out=wqb[:],
            in_=inp["wqb"].rearrange("c q r e -> (c q) r e")[
                bass.ds(me * 128, 128), :, :]).then_inc(*cx.gw.tick(16))
        nc.gpsimd.dma_start(
            out=wkk[:],
            in_=inp["wkvb_k"].rearrange("c q r e -> (c q) r e")[
                bass.ds(me * 128, 128), :, :]).then_inc(*cx.gw.tick(16))
        nc.gpsimd.dma_start(
            out=wkv[:],
            in_=inp["wkvb_v"].rearrange("c q r e -> (c q) r e")[
                bass.ds(me * 128, 128), :, :]).then_inc(*cx.gw.tick(16))
        ld = {}
        used = {}
        for ch in range(32):
            j, b_ = divmod(ch, 4)
            buf = ch % 2
            c0 = ch * 512
            if ch >= 2:
                nc.sync.wait_ge(cx.pe.h, used.get(ch - 2, 0))
            for rt in range(4):
                nc.sync.dma_start(
                    out=qa[:, buf, rt, :],
                    in_=m_all[j * 1088 + rt * 128:j * 1088 + (rt + 1) * 128,
                              b_ * 512:(b_ + 1) * 512]).then_inc(*cx.sd.tick(16))
                nc.sync.dma_start(
                    out=co[:, buf, rt, :],
                    in_=m_all[j * 1088 + 512 + rt * 128:
                              j * 1088 + 512 + (rt + 1) * 128,
                              b_ * 512:(b_ + 1) * 512]).then_inc(*cx.sd.tick(16))
            nc.tensor.wait_ge(cx.sd.h, cx.sd.n)
            if ch >= 1:
                nc.tensor.wait_ge(cx.ve.h, cx.ve.n)
            vcopy = {}
            for mi, (m0, msz, bank) in enumerate(
                    [(0, 128, 0), (128, 128, 1)]):
                for rt in range(4):
                    ins = nc.tensor.matmul(
                        psN[0:msz, mi, :], wqb[:, rt, m0:m0 + msz],
                        qa[:, buf, rt, :], start=(rt == 0), stop=(rt == 3))
                ins.then_inc(*cx.pe.tick())
            for rt in range(4):
                ins = nc.tensor.matmul(psE[0:64, :], wqb[:, rt, 256:320],
                                       qa[:, buf, rt, :],
                                       start=(rt == 0), stop=(rt == 3))
            for rt in range(4):
                ins = nc.tensor.matmul(psE[64:128, :], wqb[:, rt, 320:384],
                                       qa[:, buf, rt, :],
                                       start=(rt == 0), stop=(rt == 3))
            ins.then_inc(*cx.pe.tick())
            for mi in range(2):
                for rt in range(4):
                    ins = nc.tensor.matmul(
                        psK[:, mi, :], wkk[:, rt, mi * 128:(mi + 1) * 128],
                        co[:, buf, rt, :], start=(rt == 0), stop=(rt == 3))
                ins.then_inc(*cx.pe.tick())
            for pt_ in range(4):
                if pt_ >= 2:
                    nc.tensor.wait_ge(cx.ve.h, vcopy[pt_ - 2])
                for rt in range(4):
                    ins = nc.tensor.matmul(
                        psV[:, pt_ % 2, 0:256],
                        co[:, buf, rt, pt_ * 128:(pt_ + 1) * 128],
                        wkv[:, rt, :], start=(rt == 0), stop=(rt == 3))
                ins.then_inc(*cx.pe.tick())
                nc.vector.wait_ge(cx.pe.h, cx.pe.n)
                nc.vector.wait_ge(cx.sd.h, cx.sd.n)
                nc.vector.tensor_copy(stg[:, 4 + pt_ % 2, 0:256],
                                      psV[:, pt_ % 2, 0:256]
                                      ).then_inc(*cx.ve.tick())
                vcopy[pt_] = cx.ve.n
                nc.sync.wait_ge(cx.ve.h, cx.ve.n)
                nc.sync.dma_start(out=bv[c0 + pt_ * 128:c0 + (pt_ + 1) * 128,
                                         0:256],
                                  in_=stg[:, 4 + pt_ % 2, 0:256]
                                  ).then_inc(*cx.sd.tick(16))
            v = VS(cx)
            v.wait_ge(cx.pe.h, cx.pe.n)
            if ch >= 1:
                v.wait_ge(cx.sd.h, cx.sd.n)
            v.tensor_copy(stg[:, 0, :], psN[:, 0, :])
            v.tensor_copy(stg[:, 1, :], psN[:, 1, :])
            v.tensor_copy(stg[:, 2, :], psK[:, 0, :])
            v.tensor_copy(stg[:, 3, :], psK[:, 1, :])
            nc.vector.wait_ge(cx.gw.h, cx.gw.n)
            cos2 = mcos[:, j * 512:(j + 1) * 512]
            sin2 = msin[:, j * 512:(j + 1) * 512]
            v.tensor_mul(rop[:, 0, :], psE[0:64, :], cos2)
            v.tensor_mul(rop[:, 1, :], psE[64:128, :], sin2)
            v.tensor_sub(rop[:, 0, :], rop[:, 0, :], rop[:, 1, :])
            v.tensor_mul(rop[:, 1, :], psE[0:64, :], sin2)
            v.tensor_mul(rop[:, 2, :], psE[64:128, :], cos2)
            v.tensor_add(rop[:, 1, :], rop[:, 1, :], rop[:, 2, :])
            used[ch] = cx.pe.n
            nc.sync.wait_ge(cx.ve.h, cx.ve.n)
            nc.sync.dma_start(out=bq[0, 0:128, c0:c0 + 512],
                              in_=stg[:, 0, :]).then_inc(*cx.sd.tick(16))
            nc.sync.dma_start(out=bq[1, 0:128, c0:c0 + 512],
                              in_=stg[:, 1, :]).then_inc(*cx.sd.tick(16))
            nc.sync.dma_start(out=kn[0, :, c0:c0 + 512],
                              in_=stg[:, 2, :]).then_inc(*cx.sd.tick(16))
            nc.sync.dma_start(out=kn[1, :, c0:c0 + 512],
                              in_=stg[:, 3, :]).then_inc(*cx.sd.tick(16))
            nc.gpsimd.wait_ge(cx.ve.h, cx.ve.n)
            nc.gpsimd.dma_start(out=bq[0, 128:160, c0:c0 + 512],
                                in_=rop[0:32, 0, :]).then_inc(*cx.gw.tick(16))
            nc.gpsimd.dma_start(out=bq[1, 128:160, c0:c0 + 512],
                                in_=rop[32:64, 0, :]).then_inc(*cx.gw.tick(16))
            nc.gpsimd.dma_start(out=bq[0, 160:192, c0:c0 + 512],
                                in_=rop[0:32, 1, :]).then_inc(*cx.gw.tick(16))
            nc.gpsimd.dma_start(out=bq[1, 160:192, c0:c0 + 512],
                                in_=rop[32:64, 1, :]).then_inc(*cx.gw.tick(16))
        nc.sync.wait_ge(cx.sd.h, cx.sd.n)
        cx.barrier()


def emit_attn_inner_b(cx, consts, bq, kn, bv, m_all, bz_in):
    nc = cx.nc
    sc = 1.0 / math.sqrt(192.0)
    with ExitStack() as es:
        kb_ = es.enter_context(nc.sbuf_tensor(un("ab_k"), [128, 2, 32, 128], BF16))
        kpe = es.enter_context(nc.sbuf_tensor(un("ab_kpe"), [64, 2, 32, 128], BF16))
        qb_ = es.enter_context(nc.sbuf_tensor(un("ab_q"), [128, 2, 32, 128], BF16))
        qpe = es.enter_context(nc.sbuf_tensor(un("ab_qpe"), [64, 2, 32, 128], BF16))
        vb_ = es.enter_context(nc.sbuf_tensor(un("ab_v"), [128, 2, 32, 128], BF16))
        pt = es.enter_context(nc.sbuf_tensor(un("ab_p"), [128, 2, 512], BF16))
        sC = es.enter_context(nc.sbuf_tensor(un("ab_c"), [128, 1], F32))
        sml = es.enter_context(nc.sbuf_tensor(un("ab_sml"), [128, 4], F32))
        qss = es.enter_context(nc.sbuf_tensor(un("ab_qss"), [128, 4, 128], BF16))
        rcp = es.enter_context(nc.sbuf_tensor(un("ab_rcp"), [128, 512], F32))
        zt = es.enter_context(nc.sbuf_tensor(un("ab_z"), [128, 2, 512], BF16))
        psS = es.enter_context(nc.psum_tensor(un("ab_psS"), [128, 2, 512], F32))
        psO = es.enter_context(nc.psum_tensor(un("ab_psO"), [128, 512], F32))
        psB = es.enter_context(nc.psum_tensor(un("ab_psB"), [128, 512], F32))
        psR = es.enter_context(nc.psum_tensor(un("ab_psR"), [128, 128], F32))
        psN = es.enter_context(nc.psum_tensor(un("ab_psN"), [1, 512], F32))
        ones1 = consts["ones1"]
        onesr = consts["onesr_f"]
        diag = consts["diag"]
        pe_end = {}
        for inst in range(8):
            b, hp = divmod(inst, 2)
            buf = inst % 2
            if inst >= 2:
                nc.sync.wait_ge(cx.pe.h, pe_end[inst - 2])
            for j in range(NC):
                cs = j * 2048 + b * 512
                nc.sync.dma_start(
                    out=qb_[:, buf, j * 4:(j + 1) * 4, :].rearrange(
                        "p c e -> p (c e)"),
                    in_=bq[hp, 0:128, cs:cs + 512]).then_inc(*cx.sd.tick(16))
                nc.sync.dma_start(
                    out=qpe[:, buf, j * 4:(j + 1) * 4, :].rearrange(
                        "p c e -> p (c e)"),
                    in_=bq[hp, 128:192, cs:cs + 512]).then_inc(*cx.sd.tick(16))
                nc.sync.dma_start(
                    out=kb_[:, buf, j * 4:(j + 1) * 4, :].rearrange(
                        "p c e -> p (c e)"),
                    in_=kn[hp, :, cs:cs + 512]).then_inc(*cx.sd.tick(16))
                nc.sync.dma_start(
                    out=kpe[:, buf, j * 4:(j + 1) * 4, :].rearrange(
                        "p c e -> p (c e)"),
                    in_=m_all[j * 1088 + 1024:j * 1088 + 1088,
                              b * 512:(b + 1) * 512]).then_inc(*cx.sd.tick(16))
                for cc_ in range(4):
                    r0 = j * 2048 + b * 512 + cc_ * 128
                    nc.sync.dma_start(
                        out=vb_[:, buf, j * 4 + cc_, :],
                        in_=bv[r0:r0 + 128, hp * 128:(hp + 1) * 128]
                    ).then_inc(*cx.sd.tick(16))
            ld_n = cx.sd.n
            nc.vector.wait_ge(cx.sd.h, ld_n)
            if inst >= 1:
                nc.vector.wait_ge(cx.pe.h, cx.pe.n)
            nc.vector.tensor_copy(qss[:, 0, :], qb_[:, buf, :, 0:4])
            nc.vector.tensor_copy(qss[:, 1, :], kb_[:, buf, :, 0:4])
            nc.vector.tensor_copy(qss[0:64, 2, :], qpe[:, buf, :, 0:4])
            nc.vector.tensor_copy(qss[0:64, 3, :], kpe[:, buf, :, 0:4]
                                  ).then_inc(*cx.ve.tick())
            nc.tensor.wait_ge(cx.sd.h, ld_n)
            nc.tensor.wait_ge(cx.ve.h, cx.ve.n)
            nc.tensor.matmul(psR[:], qss[:, 0, :], qss[:, 1, :],
                             start=True, stop=False, skip_group_check=True)
            nc.tensor.matmul(psR[:], qss[0:64, 2, :], qss[0:64, 3, :],
                             start=False, stop=True, skip_group_check=True
                             ).then_inc(*cx.pe.tick())
            nc.vector.wait_ge(cx.pe.h, cx.pe.n)
            nc.vector.tensor_reduce(sml[:, 0:1], psR[:], AX.X, ALU.max
                                    ).then_inc(*cx.ve.tick())
            nc.tensor.wait_ge(cx.ve.h, cx.ve.n)
            nc.tensor.transpose(psR[0:1, :], sml[:, 0:1], consts["ident_f"][:]
                                ).then_inc(*cx.pe.tick())
            nc.vector.wait_ge(cx.pe.h, cx.pe.n)
            nc.vector.tensor_reduce(sml[0:1, 1:2], psR[0:1, :], AX.X, ALU.max)
            nc.vector.tensor_scalar(sml[0:1, 2:3], sml[0:1, 1:2], -sc, -2.0 * sc,
                                    ALU.mult, ALU.add).then_inc(*cx.ve.tick())
            nc.tensor.wait_ge(cx.ve.h, cx.ve.n)
            nc.tensor.matmul(psR[:, 0:1], onesr[:], sml[0:1, 2:3],
                             start=True, stop=True).then_inc(*cx.pe.tick())
            nc.vector.wait_ge(cx.pe.h, cx.pe.n)
            nc.vector.wait_ge(cx.ac.h, cx.ac.n)
            nc.vector.tensor_copy(sC[:], psR[:, 0:1]).then_inc(*cx.ve.tick())
            nc.scalar.wait_ge(cx.ve.h, cx.ve.n)
            for jq in range(8):
                nk = 4 * jq + 4
                qrhs = qb_[:, buf, jq * 4:(jq + 1) * 4, :].rearrange(
                    "p c e -> p (c e)")
                qprhs = qpe[:, buf, jq * 4:(jq + 1) * 4, :].rearrange(
                    "p c e -> p (c e)")
                st_tick = {}
                exp_tick = {}
                avs_tick = {}
                ac_base = cx.ac.n

                def emit_st(k):
                    sb = k % 2
                    if k >= 2:
                        nc.tensor.wait_ge(cx.ac.h, exp_tick[k - 2])
                    else:
                        nc.tensor.wait_ge(cx.ac.h, ac_base)
                    nc.tensor.matmul(
                        psS[:, sb, :], kb_[:, buf, k, :], qrhs,
                        start=True, stop=False, skip_group_check=True)
                    nc.tensor.matmul(
                        psS[:, sb, :], kpe[:, buf, k, :], qprhs,
                        start=False, stop=True, skip_group_check=True
                    ).then_inc(*cx.pe.tick())
                    st_tick[k] = cx.pe.n

                emit_st(0)
                for k in range(nk):
                    sb = k % 2
                    if k + 1 < nk:
                        emit_st(k + 1)
                    v_of_k = k - 4 * jq
                    if 0 <= v_of_k < 4:
                        nc.vector.wait_ge(cx.pe.h, st_tick[k])
                        nc.vector.tensor_tensor(
                            psS[:, sb, :], psS[:, sb, :], diag[:, v_of_k, :],
                            ALU.add).then_inc(*cx.ve.tick())
                        nc.scalar.wait_ge(cx.ve.h, cx.ve.n)
                    else:
                        nc.scalar.wait_ge(cx.pe.h, st_tick[k])
                    if k >= 2:
                        nc.scalar.wait_ge(cx.pe.h, avs_tick[k - 2])
                    nc.scalar.activation(pt[:, sb, :], psS[:, sb, :], AF.Exp,
                                         bias=sC[:, 0:1], scale=sc
                                         ).then_inc(*cx.ac.tick())
                    exp_tick[k] = cx.ac.n
                    nc.tensor.wait_ge(cx.ac.h, exp_tick[k])
                    nc.tensor.matmul(psO[:], vb_[:, buf, k, :], pt[:, sb, :],
                                     start=(k == 0), stop=(k == nk - 1),
                                     skip_group_check=True)
                    nc.tensor.matmul(psN[:], ones1[:], pt[:, sb, :],
                                     start=(k == 0), stop=(k == nk - 1),
                                     skip_group_check=True).then_inc(*cx.pe.tick())
                    avs_tick[k] = cx.pe.n
                nc.vector.wait_ge(cx.pe.h, cx.pe.n)
                nc.vector.reciprocal(rcp[0:1, :], psN[:]).then_inc(*cx.ve.tick())
                nc.tensor.wait_ge(cx.ve.h, cx.ve.n)
                nc.tensor.matmul(psB[:], onesr[:], rcp[0:1, :],
                                 start=True, stop=True).then_inc(*cx.pe.tick())
                nc.vector.wait_ge(cx.pe.h, cx.pe.n)
                if jq >= 2:
                    nc.vector.wait_ge(cx.sd.h, cx.sd.n)
                nc.vector.tensor_copy(rcp[:], psB[:])
                nc.vector.tensor_mul(zt[:, jq % 2, :], psO[:], rcp[:]
                                     ).then_inc(*cx.ve.tick())
                nc.sync.wait_ge(cx.ve.h, cx.ve.n)
                nc.sync.dma_start(
                    out=bz_in[hp * 128:(hp + 1) * 128,
                              b * 4096 + jq * 512:b * 4096 + (jq + 1) * 512],
                    in_=zt[:, jq % 2, :]).then_inc(*cx.sd.tick(16))
                nc.tensor.wait_ge(cx.ve.h, cx.ve.n)
            pe_end[inst] = cx.pe.n
        nc.sync.wait_ge(cx.sd.h, cx.sd.n)
        cx.barrier()


def stage_inputs(inputs):
    x = inputs["L0_attn_a_x"]
    perm64 = np.concatenate([np.arange(0, 64, 2), np.arange(1, 64, 2)])

    common = {}
    common["wq_a"] = np.ascontiguousarray(
        inputs["L0_attn_a_W_Q"].reshape(H, T, 128, 128))
    common["wk_a"] = np.ascontiguousarray(
        inputs["L0_attn_a_W_K"].reshape(H, T, 128, 128))
    common["wv_a"] = np.ascontiguousarray(
        inputs["L0_attn_a_W_V"].transpose(1, 0, 2).reshape(T, 128, D))
    common["wo_a"] = np.ascontiguousarray(
        inputs["L0_attn_a_W_O"].reshape(H, 128, T, 128).transpose(0, 2, 1, 3))
    common["bq_a"] = np.ascontiguousarray(inputs["L0_attn_a_b_Q"].T)
    common["bk_a"] = np.ascontiguousarray(inputs["L0_attn_a_b_K"].T)
    common["bv_a"] = np.ascontiguousarray(
        np.broadcast_to(inputs["L0_attn_a_b_V"].reshape(1, D), (128, D)))
    common["bo_a"] = np.ascontiguousarray(
        inputs["L0_attn_a_b_O"].reshape(T, 128).T)

    common["wqa"] = np.ascontiguousarray(
        inputs["L0_attn_b_Wqa"].reshape(T, 128, 4, 128).transpose(0, 2, 1, 3))
    wqb = inputs["L0_attn_b_Wqb"].reshape(512, H, MLA_QH)
    qb = np.empty((NC, 512, 384), np.float32)
    for pr in range(NC):
        h0, h1 = 2 * pr, 2 * pr + 1
        qb[pr, :, 0:128] = wqb[:, h0, :128]
        qb[pr, :, 128:256] = wqb[:, h1, :128]
        qb[pr, :, 256:288] = wqb[:, h0, 128:][:, perm64[:32]]
        qb[pr, :, 288:320] = wqb[:, h1, 128:][:, perm64[:32]]
        qb[pr, :, 320:352] = wqb[:, h0, 128:][:, perm64[32:]]
        qb[pr, :, 352:384] = wqb[:, h1, 128:][:, perm64[32:]]
    common["wqb"] = np.ascontiguousarray(
        qb.reshape(NC, 4, 128, 384).transpose(0, 2, 1, 3))
    wkva = inputs["L0_attn_b_Wkva"].copy()
    wkva[:, 512:] = wkva[:, 512:][:, perm64]
    common["wkva"] = np.ascontiguousarray(wkva.reshape(T, 128, 576))
    wkvb = inputs["L0_attn_b_Wkvb"].reshape(512, H, 256)
    kb = np.empty((NC, 512, 256), np.float32)
    vb = np.empty((NC, 512, 256), np.float32)
    for pr in range(NC):
        h0, h1 = 2 * pr, 2 * pr + 1
        kb[pr, :, 0:128] = wkvb[:, h0, :128]
        kb[pr, :, 128:256] = wkvb[:, h1, :128]
        vb[pr, :, 0:128] = wkvb[:, h0, 128:]
        vb[pr, :, 128:256] = wkvb[:, h1, 128:]
    common["wkvb_k"] = np.ascontiguousarray(
        kb.reshape(NC, 4, 128, 256).transpose(0, 2, 1, 3))
    common["wkvb_v"] = np.ascontiguousarray(
        vb.reshape(NC, 4, 128, 256).transpose(0, 2, 1, 3))
    common["wout_b"] = np.ascontiguousarray(
        inputs["L0_attn_b_Wout"].reshape(H, 128, T, 128).transpose(0, 2, 1, 3))

    common["f_conv"] = np.ascontiguousarray(
        inputs["L0_ffn_conv_w"].reshape(T, 128, 2))
    common["f_wq"] = np.ascontiguousarray(
        inputs["L0_ffn_Wq"].reshape(T, 128, FDK))
    common["f_wk"] = np.ascontiguousarray(
        inputs["L0_ffn_Wk"].reshape(T, 128, FDK))
    common["f_wv"] = np.ascontiguousarray(
        inputs["L0_ffn_Wv"].reshape(T, 128, D))
    common["f_wg"] = np.ascontiguousarray(
        inputs["L0_ffn_Wg"].reshape(T, 128, T, 128))
    common["f_wout"] = np.ascontiguousarray(
        inputs["L0_ffn_Wout"].reshape(T, 128, T, 128))
    common["f_rw"] = np.ascontiguousarray(inputs["L0_ffn_router_w"])

    common["c_ident"] = np.eye(128, dtype=np.float32)
    theta = 10000.0 ** (-2.0 * np.arange(32, dtype=np.float64) / 64.0)
    ang = np.arange(S, dtype=np.float64)[:, None] * theta[None, :]
    cosT = np.cos(ang).astype(np.float32).T
    sinT = np.sin(ang).astype(np.float32).T
    common["mla_cos2"] = np.ascontiguousarray(np.concatenate([cosT, cosT], 0))
    common["mla_sin2"] = np.ascontiguousarray(np.concatenate([sinT, sinT], 0))
    inv_freq = 1.0 / 10000.0 ** (np.arange(0, FHDK, 2, dtype=np.float64) / FHDK)
    fang = np.arange(B, dtype=np.float64)[:, None] * inv_freq[None, :]
    fc = np.cos(fang).astype(np.float32)
    fs = np.sin(fang).astype(np.float32)
    common["ffn_cos"] = np.ascontiguousarray(
        np.broadcast_to(np.tile(fc, (1, H))[:, None, :], (B, 128, H * 32)))
    common["ffn_sin"] = np.ascontiguousarray(
        np.broadcast_to(np.tile(fs, (1, H))[:, None, :], (B, 128, H * 32)))
    kk = np.arange(128)[:, None]
    qq = np.arange(512)[None, :]
    mvar = np.stack([np.where(kk + 128 * v <= qq, 0.0, NEGBIG)
                     for v in range(4)]).astype(np.float32)
    common["c_diag"] = np.ascontiguousarray(mvar)

    in_maps = []
    for c in range(NC):
        m = dict(common)
        xs = x[:, c * SL:(c + 1) * SL, :]
        m["x_in"] = np.ascontiguousarray(
            xs.transpose(2, 0, 1).reshape(T, 128, RW))
        sl = slice(c * SL, (c + 1) * SL)
        m["kpe_cos"] = np.ascontiguousarray(cosT[:, sl])
        m["kpe_sin"] = np.ascontiguousarray(sinT[:, sl])
        in_maps.append(m)
    return in_maps


def declare_inputs(nc):
    d = {}
    d["x_in"] = dram(nc, "x_in", [T, 128, RW], F32, "ExternalInput")
    for nm, shape in [
        ("wq_a", [H, T, 128, 128]), ("wk_a", [H, T, 128, 128]),
        ("wv_a", [T, 128, D]), ("wo_a", [H, T, 128, 128]),
        ("bq_a", [128, H]), ("bk_a", [128, H]), ("bv_a", [128, D]),
        ("bo_a", [128, T]),
        ("wqa", [T, 4, 128, 128]), ("wqb", [NC, 128, 4, 384]),
        ("wkva", [T, 128, 576]), ("wkvb_k", [NC, 128, 4, 256]),
        ("wkvb_v", [NC, 128, 4, 256]), ("wout_b", [H, T, 128, 128]),
        ("f_conv", [T, 128, 2]), ("f_wq", [T, 128, FDK]), ("f_wk", [T, 128, FDK]),
        ("f_wv", [T, 128, D]), ("f_wg", [T, 128, T, 128]),
        ("f_wout", [T, 128, T, 128]), ("f_rw", [H, 128, 2]),
        ("c_ident", [128, 128]),
        ("mla_cos2", [64, S]), ("mla_sin2", [64, S]),
        ("ffn_cos", [B, 128, H * 32]), ("ffn_sin", [B, 128, H * 32]),
        ("c_diag", [4, 128, 512]),
        ("kpe_cos", [32, SL]), ("kpe_sin", [32, SL]),
    ]:
        d[nm] = dram(nc, nm, shape, F32, "ExternalInput")
    return d


def load_consts(cx, inp):
    nc = cx.nc
    c = {}
    c["ones1"] = cx.sb("c_ones1", [128, 1], BF16)
    c["onesr"] = cx.sb("c_onesr", [1, 128], BF16)
    c["onesr_f"] = cx.sb("c_onesr_f", [1, 128], F32)
    c["ident"] = cx.sb("c_identb", [128, 128], BF16)
    c["ident_f"] = cx.sb("c_identf", [128, 128], F32)
    c["diag"] = cx.sb("c_diag_s", [128, 4, 512], F32)
    nc.vector.memset(c["ones1"][:], 1.0)
    nc.vector.memset(c["onesr"][:], 1.0)
    nc.vector.memset(c["onesr_f"][:], 1.0)
    nc.gpsimd.dma_start(out=c["ident"][:], in_=inp["c_ident"]
                        ).then_inc(*cx.gw.tick(16))
    nc.gpsimd.dma_start(out=c["ident_f"][:], in_=inp["c_ident"]
                        ).then_inc(*cx.gw.tick(16))
    nc.gpsimd.dma_start(
        out=c["diag"][:],
        in_=inp["c_diag"].rearrange("v k q -> k v q")).then_inc(*cx.gw.tick(16))
    nc.gpsimd.wait_ge(cx.gw.h, cx.gw.n)
    return c


def build(taps=()):
    nc = bass.Bass("TRN2", target_bir_lowering=False)
    inp = declare_inputs(nc)
    out = dram(nc, "out", [T, 128, RW], F32, "ExternalOutput")
    tap_outs = {t: dram(nc, "tap_" + t, [T, 128, RW], F32, "ExternalOutput")
                for t in taps if t.startswith("x")}
    ln_taps = {t: dram(nc, "tap_" + t, [T, 128, RW], F32, "ExternalOutput")
               for t in taps if t.startswith("ln")}
    misc_taps = {}
    misc_outs = {}

    xres = dram(nc, "xres", [T, 128, RW], BF16)
    aq_in = dram(nc, "aq_in", [D, RW], BF16)
    ak_in = dram(nc, "ak_in", [D, RW], BF16)
    av_in = dram(nc, "av_in", [RW, D], BF16)
    aq_all = dram(nc, "aq_all", [NC * D, RW], BF16)
    ak_all = dram(nc, "ak_all", [NC * D, RW], BF16)
    av_all = dram(nc, "av_all", [NC * RW, D], BF16)
    az_in = dram(nc, "az_in", [HPC * 128, B * S], BF16)
    az_all = dram(nc, "az_all", [D, B * S], BF16)
    aq_my = dram(nc, "aq_my", [NC, HPC * 128, RW], BF16)
    ak_my = dram(nc, "ak_my", [NC, HPC * 128, RW], BF16)
    av_my = dram(nc, "av_my", [NC, RW, HPC * 128], BF16)
    az_my = dram(nc, "az_my", [D, B, 512], BF16)
    vffn = dram(nc, "vffn", [RW, D], BF16)
    m_in = dram(nc, "m_in", [1088, RW], BF16)
    m_all = dram(nc, "m_all", [NC * 1088, RW], BF16)
    bq = dram(nc, "bq", [HPC, 192, B * S], BF16)
    kn = dram(nc, "kn", [HPC, 128, B * S], BF16)
    bv = dram(nc, "bv", [B * S, HPC * 128], BF16)
    bz_in = dram(nc, "bz_in", [HPC * 128, B * S], BF16)
    bz_all = dram(nc, "bz_all", [D, B * S], BF16)
    bz_my = dram(nc, "bz_my", [D, B, 512], BF16)
    rg = [list(range(NC))]
    if "aqkvz" in taps:
        misc_taps["aq"] = (aq_in, [D, RW])
        misc_taps["ak"] = (ak_in, [D, RW])
        misc_taps["av"] = (av_in, [RW, D])
        misc_taps["az"] = (az_in, [HPC * 128, B * S])
        for nm, (srcap, shp) in misc_taps.items():
            misc_outs[nm] = dram(nc, "tapm_" + nm, shp, F32, "ExternalOutput")

    def extract_rows(dst, src_ap, blk, width):
        me = nc.sync.partition_id()
        view = src_ap.rearrange("(j d) r -> j d r", j=NC)
        nc.sync.dma_start(out=dst[:],
                          in_=view[:, bass.ds(me * width, width), :]
                          ).then_inc(*cx.sd.tick(16))

    def extract_cols(dst, src_ap, width):
        me = nc.sync.partition_id()
        view = src_ap.rearrange("(j d) r -> j d r", j=NC)
        nc.sync.dma_start(out=dst[:],
                          in_=view[:, :, bass.ds(me * width, width)]
                          ).then_inc(*cx.sd.tick(16))

    def extract_z(dst, src_ap):
        me = nc.sync.partition_id()
        view = src_ap.rearrange("d (b p) -> d b p", b=B)
        nc.sync.dma_start(out=dst[:],
                          in_=view[:, :, bass.ds(me * 512, 512)]
                          ).then_inc(*cx.sd.tick(16))

    def tap_x(name):
        if name in tap_outs:
            nc.gpsimd.dma_start(out=tap_outs[name], in_=xres
                                ).then_inc(*cx.gw.tick(16))
            nc.gpsimd.wait_ge(cx.gw.h, cx.gw.n)
            cx.barrier()

    def tap_dram(name, src):
        if name in misc_taps:
            nc.gpsimd.dma_start(out=misc_taps[name], in_=src
                                ).then_inc(*cx.gw.tick(16))
            nc.gpsimd.wait_ge(cx.gw.h, cx.gw.n)
            cx.barrier()

    cx = Ctx(nc)
    with cx.es:
        consts = load_consts(cx, inp)
        nc.gpsimd.dma_start(out=xres, in_=inp["x_in"]).then_inc(*cx.gw.tick(16))
        nc.gpsimd.wait_ge(cx.gw.h, cx.gw.n)
        cx.barrier()

        with ExitStack() as st_es:
            st = {"ln": st_es.enter_context(
                nc.sbuf_tensor(un("ln_a"), [128, T, RW], BF16))}
            emit_ln(cx, st, xres, consts)
            cx.barrier()
            if "ln1" in ln_taps:
                for t in range(T):
                    nc.gpsimd.dma_start(out=ln_taps["ln1"][t, :, :],
                                        in_=st["ln"][:, t, :]
                                        ).then_inc(*cx.gw.tick(16))
                nc.gpsimd.wait_ge(cx.gw.h, cx.gw.n)
                cx.barrier()
            emit_qkv_a(cx, st["ln"], inp, aq_in, ak_in, av_in)
        nc.gpsimd.collective_compute("AllGather", ALU.bypass, replica_groups=rg,
                                     ins=[aq_in.opt()], outs=[aq_all.opt()]
                                     ).then_inc(*cx.cc.tick())
        nc.gpsimd.collective_compute("AllGather", ALU.bypass, replica_groups=rg,
                                     ins=[ak_in.opt()], outs=[ak_all.opt()]
                                     ).then_inc(*cx.cc.tick())
        nc.gpsimd.collective_compute("AllGather", ALU.bypass, replica_groups=rg,
                                     ins=[av_in.opt()], outs=[av_all.opt()]
                                     ).then_inc(*cx.cc.tick())
        nc.gpsimd.wait_ge(cx.cc.h, cx.cc.n)
        cx.barrier()
        extract_rows(aq_my, aq_all, D, HPC * 128)
        extract_rows(ak_my, ak_all, D, HPC * 128)
        extract_cols(av_my, av_all, HPC * 128)
        nc.sync.wait_ge(cx.sd.h, cx.sd.n)
        cx.barrier()
        emit_attn_inner_a(cx, consts, aq_my, ak_my, av_my, az_in)
        nc.gpsimd.collective_compute("AllGather", ALU.bypass, replica_groups=rg,
                                     ins=[az_in.opt()], outs=[az_all.opt()]
                                     ).then_inc(*cx.cc.tick())
        nc.gpsimd.wait_ge(cx.cc.h, cx.cc.n)
        cx.barrier()
        extract_z(az_my, az_all)
        nc.sync.wait_ge(cx.sd.h, cx.sd.n)
        cx.barrier()
        if misc_taps:
            for nm, (srcap, shp) in misc_taps.items():
                nc.gpsimd.dma_start(out=misc_outs[nm], in_=srcap
                                    ).then_inc(*cx.gw.tick(16))
            nc.gpsimd.wait_ge(cx.gw.h, cx.gw.n)
            cx.barrier()
        emit_outproj(cx, inp["wo_a"], inp["bo_a"], az_my, xres)
        tap_x("x1")

        fdbg = {}
        if "ffndbg" in taps:
            for nm, shp in [("fq", [128, T, FDK]), ("fk", [128, T, FDK]),
                            ("fi0", [128, 16, H]), ("fal", [128, 16, H]),
                            ("fi0b", [128, 16, H]), ("fi0c", [128, 16, H]),
                            ("falc", [128, 16, H]),
                            ("fcf", [128, 4, 4, 4, H]), ("ftd", [128, 16, H]),
                            ("folnT", [128, T, RW]), ("folnG", [128, T, RW])]:
                fdbg[nm] = dram(nc, "tapf_" + nm, shp, F32, "ExternalOutput")
            misc_taps["fv"] = (vffn, [RW, D])
            misc_outs["fv"] = dram(nc, "tapm_fv", [RW, D], F32, "ExternalOutput")
        emit_ffn(cx, consts, inp, xres, vffn, dbg=fdbg)
        if "ffndbg" in taps:
            nc.gpsimd.dma_start(out=misc_outs["fv"], in_=vffn
                                ).then_inc(*cx.gw.tick(16))
            nc.gpsimd.wait_ge(cx.gw.h, cx.gw.n)
            cx.barrier()
        tap_x("x2")

        with ExitStack() as st_es:
            st = {"ln": st_es.enter_context(
                nc.sbuf_tensor(un("ln_b"), [128, T, RW], BF16))}
            emit_ln(cx, st, xres, consts)
            cx.barrier()
            if "ln3" in ln_taps:
                for t in range(T):
                    nc.gpsimd.dma_start(out=ln_taps["ln3"][t, :, :],
                                        in_=st["ln"][:, t, :]
                                        ).then_inc(*cx.gw.tick(16))
                nc.gpsimd.wait_ge(cx.gw.h, cx.gw.n)
                cx.barrier()
            emit_mla_proj(cx, st["ln"], inp, m_in)
        nc.gpsimd.collective_compute("AllGather", ALU.bypass, replica_groups=rg,
                                     ins=[m_in.opt()], outs=[m_all.opt()]
                                     ).then_inc(*cx.cc.tick())
        nc.gpsimd.wait_ge(cx.cc.h, cx.cc.n)
        cx.barrier()
        emit_mla_expand(cx, consts, inp, m_all, bq, kn, bv)
        emit_attn_inner_b(cx, consts, bq, kn, bv, m_all, bz_in)
        nc.gpsimd.collective_compute("AllGather", ALU.bypass, replica_groups=rg,
                                     ins=[bz_in.opt()], outs=[bz_all.opt()]
                                     ).then_inc(*cx.cc.tick())
        nc.gpsimd.wait_ge(cx.cc.h, cx.cc.n)
        cx.barrier()
        extract_z(bz_my, bz_all)
        nc.sync.wait_ge(cx.sd.h, cx.sd.n)
        cx.barrier()
        emit_outproj(cx, inp["wout_b"], None, bz_my, xres)
        tap_x("x3")

        emit_ffn(cx, consts, inp, xres, vffn, out_fp32=out)
        tap_x("x4")
    return nc


def kernel(**inputs):
    in_maps = stage_inputs(inputs)
    nc = build()
    res = run_bass_kernel_spmd(nc, in_maps, core_ids=list(range(NC)))
    return assemble(res, "out")


def assemble(res, name):
    parts = []
    for c in range(NC):
        a = res.results[c][name]
        parts.append(a.reshape(D, B, SL).transpose(1, 2, 0))
    return np.concatenate(parts, axis=1)
